# version 03
# baseline (speedup 1.0000x reference)
from contextlib import ExitStack

import numpy as np
import ml_dtypes

import concourse.bass as bass
import concourse.mybir as mybir
from concourse.bass_utils import run_bass_kernel_spmd

F32 = mybir.dt.float32
BF16 = mybir.dt.bfloat16
AF = mybir.ActivationFunctionType
ALU = mybir.AluOpType
AX = mybir.AxisListType

NT = 4096
C = 1024
NH = 16
HD = 64
NPAIR = 8
NTB = 8
TB = 512
NST = 32
QW = 512
NE = 8
FF = 4096
SHIFT = -16.0
EPS = 1e-5

BF = ml_dtypes.bfloat16

_CACHE = {}


_TABLE_PROG = r"""
import numpy as np, sys
import jax.numpy as jnp
NT, NH, HD = 4096, 16, 64
inv_freq = 1.0 / (10000.0 ** (jnp.arange(0, HD, 2, dtype=jnp.float32) / HD))
t = jnp.arange(NT, dtype=jnp.float32)
cos_t, sin_t = [], []
for i in range(NH):
    freq_scale = 10000.0 ** (i / NH)
    freqs = jnp.outer(t, inv_freq * freq_scale)
    cos_t.append(np.asarray(jnp.cos(freqs)).T.astype(np.float32))
    sin_t.append(np.asarray(jnp.sin(freqs)).T.astype(np.float32))
np.stack([np.stack(cos_t), np.stack(sin_t)]).tofile(sys.argv[1])
"""


def _rope_tables_cpu():
    import os, subprocess, sys, tempfile
    fd, path = tempfile.mkstemp(suffix=".bin")
    os.close(fd)
    try:
        env = {**os.environ, "JAX_PLATFORMS": "cpu", "PYTHONPATH": ""}
        subprocess.run([sys.executable, "-c", _TABLE_PROG, path],
                       env=env, check=True, capture_output=True, timeout=600)
        z = np.fromfile(path, dtype=np.float32).reshape(2, NH, 32, NT)
        return [z[0][i] for i in range(NH)], [z[1][i] for i in range(NH)]
    finally:
        os.unlink(path)


def _host_constants():
    if "const" in _CACHE:
        return _CACHE["const"]
    cos_t, sin_t = _rope_tables_cpu()
    rope_a = np.zeros((NPAIR, 128, NT), dtype=np.float32)
    rope_b = np.zeros((NPAIR, 128, NT), dtype=np.float32)
    for p in range(NPAIR):
        h0, h1 = 2 * p, 2 * p + 1
        rope_a[p, 0:32] = cos_t[h0]
        rope_a[p, 32:64] = cos_t[h0]
        rope_a[p, 64:96] = cos_t[h1]
        rope_a[p, 96:128] = cos_t[h1]
        rope_b[p, 0:32] = -sin_t[h0]
        rope_b[p, 32:64] = sin_t[h0]
        rope_b[p, 64:96] = -sin_t[h1]
        rope_b[p, 96:128] = sin_t[h1]

    dmask = np.zeros((128, 4 * QW), dtype=np.float32)
    for mp in range(4):
        for s in range(128):
            q0 = mp * 128 + s
            if q0 < QW:
                dmask[s, mp * QW + q0:(mp + 1) * QW] = 1.0
    sel65 = np.zeros((65, 128), dtype=np.float32)
    sel65[64, :] = 1.0
    sel8 = np.zeros((8, 8 * 128), dtype=np.float32)
    for e in range(8):
        sel8[e, e * 128:(e + 1) * 128] = 1.0
    iden2 = np.zeros((128, 64), dtype=np.float32)
    iden2[0:64] = np.eye(64, dtype=np.float32)
    iden2[64:128] = np.eye(64, dtype=np.float32)
    idenb = np.eye(128, dtype=np.float32)
    c = dict(rope_a=rope_a, rope_b=rope_b, dmask=dmask,
             sel65=sel65, sel8=sel8, iden2=iden2,
             idenb=idenb, idenf=idenb.astype(np.float32),
             ones_r=np.ones((1, 128), dtype=np.float32))
    _CACHE["const"] = c
    return c


def _arrange_weights(Wqkv, Wproj, cascade_w, wr, br, w1, b1, w2, b2):
    ev = np.arange(0, HD, 2)
    od = np.arange(1, HD, 2)
    tiles = []
    for p in range(NPAIR):
        h0, h1 = 2 * p, 2 * p + 1
        q0, q1 = h0 * HD, h1 * HD
        k0, k1 = C + h0 * HD, C + h1 * HD
        kx = np.concatenate([Wqkv[:, k0 + ev], Wqkv[:, k0 + od],
                             Wqkv[:, k1 + ev], Wqkv[:, k1 + od]], axis=1)
        ky = np.concatenate([Wqkv[:, k0 + od], Wqkv[:, k0 + ev],
                             Wqkv[:, k1 + od], Wqkv[:, k1 + ev]], axis=1)
        qx = np.concatenate([Wqkv[:, q0 + ev], Wqkv[:, q0 + od],
                             Wqkv[:, q1 + ev], Wqkv[:, q1 + od]], axis=1)
        qy = np.concatenate([Wqkv[:, q0 + od], Wqkv[:, q0 + ev],
                             Wqkv[:, q1 + od], Wqkv[:, q1 + ev]], axis=1)
        tiles.append(np.concatenate([kx, ky, qx, qy], axis=1))
    return dict(
        wqk=np.ascontiguousarray(np.concatenate(tiles, axis=1)),
        wv=np.ascontiguousarray(Wqkv[:, 2 * C:3 * C]),
        wproj=np.ascontiguousarray(Wproj),
        wr_arr=np.ascontiguousarray(
            wr.reshape(8, 128, 8).transpose(1, 0, 2).reshape(128, 64)),
        br_arr=np.ascontiguousarray(br.reshape(1, 8)),
        b1_arr=np.ascontiguousarray(b1.reshape(8, 32, 128).transpose(0, 2, 1)),
        b2_arr=np.ascontiguousarray(b2.reshape(8, 8, 128).transpose(0, 2, 1)),
        casc=np.ascontiguousarray(np.broadcast_to(
            np.asarray(cascade_w).reshape(1, 15), (128, 15))),
        w1=np.ascontiguousarray(w1), w2=np.ascontiguousarray(w2))


ENGS = ("tensor", "scalar", "vector", "sync", "gpsimd")


class Plan:
    def __init__(self, sems):
        self.sems = sems
        self.cnt = {k: 0 for k in sems}
        self.ops = {e: [] for e in ENGS}
        self.last_wait = {e: {} for e in ENGS}

    def wait(self, eng, sem, val=None):
        v = self.cnt[sem] if val is None else val
        assert v <= self.cnt[sem], f"wait on future {sem}: {v} > {self.cnt[sem]}"
        if v <= 0 or self.last_wait[eng].get(sem, -1) >= v:
            return
        self.last_wait[eng][sem] = v
        h = self.sems[sem]
        self.ops[eng].append(lambda e, h=h, v=v: e.wait_ge(h, v))

    def emit(self, eng, fn, sem=None, n=1):
        if sem is None:
            self.ops[eng].append(fn)
            return None
        h = self.sems[sem]
        self.cnt[sem] += n
        self.ops[eng].append(lambda e, fn=fn, h=h, n=n: fn(e).then_inc(h, n))
        return self.cnt[sem]

    def barrier(self):
        snap = dict(self.cnt)
        for eng in ENGS:
            for s, v in snap.items():
                self.wait(eng, s, v)


class Arena:
    def __init__(self, cols):
        self.cols = cols
        self.off = 0
        self.marks = []

    def alloc(self, n):
        o = (self.off + 15) // 16 * 16
        assert o + n <= self.cols, f"arena overflow {o}+{n}>{self.cols}"
        self.off = o + n
        return o

    def push(self):
        self.marks.append(self.off)

    def pop(self):
        self.off = self.marks.pop()


def build(dbg=(), stop=None, skip=()):
    nc = bass.Bass("TRN2", target_bir_lowering=False)

    d_x = nc.dram_tensor("x", [NT, C], F32, kind="ExternalInput")
    d_wqk = nc.dram_tensor("wqk", [C, 4096], F32, kind="ExternalInput")
    d_wv = nc.dram_tensor("wv", [C, C], F32, kind="ExternalInput")
    d_wproj = nc.dram_tensor("wproj", [C, C], F32, kind="ExternalInput")
    d_casc = nc.dram_tensor("casc", [128, 15], F32, kind="ExternalInput")
    d_wr = nc.dram_tensor("wr_arr", [128, 64], F32, kind="ExternalInput")
    d_br = nc.dram_tensor("br_arr", [1, 8], F32, kind="ExternalInput")
    d_w1 = nc.dram_tensor("w1", [NE, C, FF], F32, kind="ExternalInput")
    d_b1 = nc.dram_tensor("b1_arr", [NE, 128, 32], F32, kind="ExternalInput")
    d_w2 = nc.dram_tensor("w2", [NE, FF, C], F32, kind="ExternalInput")
    d_b2 = nc.dram_tensor("b2_arr", [NE, 128, 8], F32, kind="ExternalInput")
    d_ra = nc.dram_tensor("rope_a", [NPAIR, 128, NT], F32, kind="ExternalInput")
    d_rbt = nc.dram_tensor("rope_b", [NPAIR, 128, NT], F32, kind="ExternalInput")
    d_dm = nc.dram_tensor("dmask", [128, 4 * QW], F32, kind="ExternalInput")
    d_s65 = nc.dram_tensor("sel65", [65, 128], F32, kind="ExternalInput")
    d_s8 = nc.dram_tensor("sel8", [8, 8 * 128], F32, kind="ExternalInput")
    d_i2 = nc.dram_tensor("iden2", [128, 64], F32, kind="ExternalInput")
    d_ib = nc.dram_tensor("idenb", [128, 128], F32, kind="ExternalInput")
    d_if = nc.dram_tensor("idenf", [128, 128], F32, kind="ExternalInput")
    d_1r = nc.dram_tensor("ones_r", [1, 128], F32, kind="ExternalInput")
    d_out = nc.dram_tensor("out", [NT, C], F32, kind="ExternalOutput")

    def scratch(name, shape, dtype):
        kind = "ExternalOutput" if name in dbg else "Internal"
        return nc.dram_tensor(name, shape, dtype, kind=kind)

    d_xb = scratch("xbf", [NT, C], BF16)
    d_wqkb = scratch("wqkb", [C, 4096], BF16)
    d_wvb = scratch("wvb", [C, C], BF16)
    d_wpb = scratch("wpb", [C, C], BF16)
    d_w1b = scratch("w1b", [NE, C, FF], BF16)
    d_w2b = scratch("w2b", [NE, FF, C], BF16)
    d_attnT = scratch("attnT", [C, NT], BF16)
    d_projT = scratch("projT", [C, NT], BF16)
    d_x2n = scratch("x2n", [NT, C], BF16)
    d_y2T = scratch("y2T", [C, NT], BF16)
    d_msk = scratch("msk", [8, NT], BF16)

    es = ExitStack()
    E = es.enter_context

    AB_COLS = 87552
    AF_COLS = 6400
    ab_t = E(nc.sbuf_tensor("arena_bf", [128, AB_COLS], BF16))
    af_t = E(nc.sbuf_tensor("arena_f32", [128, AF_COLS], F32))
    ab = Arena(AB_COLS)
    af = Arena(AF_COLS)

    s_dm = E(nc.sbuf_tensor("s_dm", [128, 4 * QW], BF16))
    s_s65 = E(nc.sbuf_tensor("s_s65", [65, 128], F32))
    s_s8 = E(nc.sbuf_tensor("s_s8", [8, 8 * 128], BF16))
    s_i2 = E(nc.sbuf_tensor("s_i2", [128, 64], BF16))
    s_ib = E(nc.sbuf_tensor("s_ib", [128, 128], BF16))
    s_if = E(nc.sbuf_tensor("s_if", [128, 128], F32))
    s_1r = E(nc.sbuf_tensor("s_1r", [1, 128], F32))
    s_casc = E(nc.sbuf_tensor("s_casc", [128, 15], F32))
    s_cascT = E(nc.sbuf_tensor("s_cascT", [128, 15], F32))
    s_wr = E(nc.sbuf_tensor("s_wr", [128, 64], F32))
    s_br = E(nc.sbuf_tensor("s_br", [1, 8], F32))

    ps_a = E(nc.psum_tensor("ps_a", [128, 1024], F32))
    ps_b = E(nc.psum_tensor("ps_b", [128, 1024], F32))
    ps_c = E(nc.psum_tensor("ps_c", [128, 512], F32))
    ps_t = E(nc.psum_tensor("ps_t", [128, 1024], BF16))
    ps_d = E(nc.psum_tensor("ps_d", [128, 512], F32))
    ps_e = E(nc.psum_tensor("ps_e", [128, 512], F32))

    sems = {}
    for name in ("TE", "SE", "VE", "QS", "QA", "QG", "ST"):
        sems[name] = E(nc.semaphore(name))
    P = Plan(sems)

    ps_free = {k: ("VE", 0) for k in
               ("a0", "a1", "b0", "b1", "c", "d", "e", "t")}

    def ps_wait(eng, *keys):
        for k in keys:
            s, v = ps_free[k]
            P.wait(eng, s, v)

    def abv(off, n, p0=0, p1=128):
        return ab_t[p0:p1, off:off + n]

    def afv(off, n, p0=0, p1=128):
        return af_t[p0:p1, off:off + n]

    def dma(q, out, in_, sem, **kw):
        return P.emit(q, lambda e, o=out, i=in_, kw=kw:
                      e.dma_start(out=o, in_=i, **kw), sem, 16)

    def mm(out, lhsT, rhs, start, stop, sem=None):
        return P.emit("tensor",
                      lambda e, o=out, l=lhsT, r=rhs, st=start, sp=stop:
                      e.matmul(o, l, r, start=st, stop=sp), sem)

    def act(out, in_, func, bias=0.0, scale=1.0, accum=None):
        return P.emit("scalar",
                      lambda e, o=out, i=in_, f=func, b=bias, s=scale, a=accum:
                      e.activation(o, i, f, bias=b, scale=s, accum_out=a), "SE")

    def vop(fn):
        return P.emit("vector", fn, "VE")

    flat2 = lambda d: d.rearrange("a b -> (a b)")[:]


    def _finish():
        P.barrier()
        with nc.Block() as block:
            for eng in ENGS:
                dec = getattr(block, eng)
                lst = P.ops[eng]

                def body(e, lst=lst):
                    for fn in lst:
                        fn(e)
                dec(body)
        es.close()
        return nc

    for (dt_, st_) in [(d_s65, s_s65), (d_if, s_if), (d_1r, s_1r)]:
        dma("sync", st_[:], dt_[:], "QS")
    for (dt_, st_) in [(d_casc, s_casc), (d_wr, s_wr), (d_br, s_br)]:
        dma("scalar", st_[:], dt_[:], "QA")
    for (dt_, st_) in [(d_dm, s_dm), (d_s8, s_s8), (d_i2, s_i2), (d_ib, s_ib)]:
        dma("gpsimd", st_[:], dt_[:], "QG")
    v_const_qs = P.cnt["QS"]
    v_const_qa = P.cnt["QA"]
    v_const_qg = P.cnt["QG"]

    dma("gpsimd", flat2(d_xb), flat2(d_x), "QG")
    v_xb = P.cnt["QG"]
    dma("gpsimd", flat2(d_wqkb), flat2(d_wqk), "QG")
    dma("gpsimd", flat2(d_wvb), flat2(d_wv), "QG")
    v_wqv = P.cnt["QG"]
    dma("gpsimd", flat2(d_wpb), flat2(d_wproj), "QG")
    v_wpb = P.cnt["QG"]
    v_w1b, v_w2b = [], []
    for e_ in range(NE):
        dma("gpsimd", flat2(d_w1b[e_]), flat2(d_w1[e_]), "QG")
        v_w1b.append(P.cnt["QG"])
        dma("gpsimd", flat2(d_w2b[e_]), flat2(d_w2[e_]), "QG")
        v_w2b.append(P.cnt["QG"])

    P.wait("vector", "QA", v_const_qa)
    vop(lambda e: e.tensor_scalar_mul(s_cascT[:], s_casc[:], 1.0 / NT))
    if stop == "phase0":
        return _finish()

    ab.push()
    af.push()
    o_P = ab.alloc(NST * 1024)
    o_rotK = ab.alloc(NT)
    o_rotQ = ab.alloc(NT)
    o_scr = [ab.alloc(5 * TB) for _ in range(2)]
    o_tmp1 = ab.alloc(TB)
    o_tmp2 = ab.alloc(TB)
    o_V = [ab.alloc(NST * 65) for _ in range(2)]
    o_attn = [ab.alloc(NT) for _ in range(2)]
    o_xc = [ab.alloc(8 * TB) for _ in range(2)]
    o_wq = [ab.alloc(5 * 1024) for _ in range(2)]
    o_rbb = ab.alloc(QW)
    o_bps = ab.alloc(QW)
    f_tab = [af.alloc(2 * TB) for _ in range(2)]
    f_o65 = af.alloc(2 * QW)
    f_bparts = af.alloc(256)
    f_bsums = af.alloc(32)
    f_bias = af.alloc(32)

    Vv = [ab_t[:, o_V[s]:o_V[s] + NST * 65].rearrange("p (m v) -> p m v", v=65)
          for s in range(2)]

    P.wait("vector", "QS", v_const_qs)
    P.wait("vector", "QG", v_const_qg)
    vop(lambda e: e.memset(afv(f_bias, 32), SHIFT))
    for s in range(2):
        vop(lambda e, s=s: e.memset(Vv[s][:, :, 64:65], 1.0))
    bias_ready_ve = P.cnt["VE"]

    xc_free = [0, 0]
    wq_free = [0, 0]
    tab_free = [0, 0]
    scr_free_te = [0, 0]
    ve_p_m = [0] * NST
    attn_store = [0, 0]
    o65_te_user = 0
    pair_end_te = 0

    for pair in range(NPAIR):
        wsl = pair % 2
        P.wait("sync", "QG", v_wqv)
        P.wait("sync", "TE", wq_free[wsl])
        wq_srcs = [(d_wqkb, pair * 512), (d_wqkb, pair * 512 + 128),
                   (d_wqkb, pair * 512 + 256), (d_wqkb, pair * 512 + 384),
                   (d_wvb, pair * 128)]
        for ti, (dsrc, coff) in enumerate(wq_srcs):
            src = dsrc.rearrange("(cc p) o -> p cc o", p=128)[:, :, coff:coff + 128]
            dma("sync", abv(o_wq[wsl] + ti * 1024, 1024), src, "QS")
        v_wq = P.cnt["QS"]

        for tb in range(NTB):
            sl = tb % 2
            P.wait("sync", "QG", v_xb)
            P.wait("sync", "TE", xc_free[sl])
            for cc in range(8):
                dma("sync", abv(o_xc[sl] + cc * TB, TB),
                    d_xb[tb * TB:(tb + 1) * TB, cc * 128:(cc + 1) * 128],
                    "QS", transpose=True)
            v_xc = P.cnt["QS"]
            P.wait("scalar", "VE", tab_free[sl])
            dma("scalar", afv(f_tab[sl], TB),
                d_ra[pair][:, tb * TB:(tb + 1) * TB], "QA")
            dma("scalar", afv(f_tab[sl] + TB, TB),
                d_rbt[pair][:, tb * TB:(tb + 1) * TB], "QA")
            v_tab = P.cnt["QA"]

            P.wait("tensor", "QS", v_wq)
            P.wait("tensor", "QS", v_xc)
            for ti in range(5) if "qkv" not in skip else []:
                half = ti % 2
                key = "a%d" % half
                pv = ps_a[:, half * 512:(half + 1) * 512]
                ps_wait("tensor", key)
                for cc in range(8):
                    mm(pv, abv(o_wq[wsl] + ti * 1024 + cc * 128, 128),
                       abv(o_xc[sl] + cc * TB, TB),
                       start=(cc == 0), stop=(cc == 7),
                       sem="TE" if cc == 7 else None)
                te_done = P.cnt["TE"]
                P.wait("scalar", "TE", te_done)
                P.wait("scalar", "TE", scr_free_te[sl])
                se_ev = act(abv(o_scr[sl] + ti * TB, TB), pv, AF.Copy)
                ps_free[key] = ("SE", se_ev)
            xc_free[sl] = P.cnt["TE"]
            if tb == NTB - 1:
                wq_free[wsl] = P.cnt["TE"]

            P.wait("vector", "QA", v_tab)
            P.wait("vector", "SE", P.cnt["SE"])
            P.wait("vector", "TE", pair_end_te)
            for kq in range(2) if "rot" not in skip else []:
                sx = o_scr[sl] + (2 * kq) * TB
                sy = o_scr[sl] + (2 * kq + 1) * TB
                dst = (o_rotK if kq == 0 else o_rotQ) + tb * TB
                P.wait("vector", "VE")
                vop(lambda e, a=sx, t=f_tab[sl]:
                    e.tensor_mul(abv(o_tmp1, TB), abv(a, TB), afv(t, TB)))
                vop(lambda e, a=sy, t=f_tab[sl] + TB:
                    e.tensor_mul(abv(o_tmp2, TB), abv(a, TB), afv(t, TB)))
                P.wait("vector", "VE")
                vop(lambda e, d=dst:
                    e.tensor_add(abv(d, TB), abv(o_tmp1, TB), abv(o_tmp2, TB)))
            tab_free[sl] = P.cnt["VE"]

            if "vtr" in skip:
                if stop == "tb0" and tb == 0:
                    return _finish()
                continue
            ps_wait("tensor", "t")
            P.wait("tensor", "SE", P.cnt["SE"])
            P.wait("tensor", "QS", v_const_qs)
            P.wait("tensor", "QG", v_const_qg)
            for side in range(2):
                for t4 in range(4):
                    slot = side * 4 + t4
                    P.emit("tensor",
                           lambda e, o=ps_t[:, slot * 64:slot * 64 + 64],
                           i=abv(o_scr[sl] + 4 * TB + t4 * 128, 128,
                                 side * 64, side * 64 + 64),
                           idn=s_i2[side * 64:side * 64 + 64, 0:64]:
                           e.transpose(o, i, idn), "TE")
                    P.wait("tensor", "TE")
            te_tr = P.cnt["TE"]
            scr_free_te[sl] = te_tr
            P.wait("vector", "TE", te_tr)
            P.wait("vector", "TE", pair_end_te)
            pst0 = ps_t[:, 0:256].rearrange("p (s x) -> p s x", x=64)
            pst1 = ps_t[:, 256:512].rearrange("p (s x) -> p s x", x=64)
            for side in range(2):
                src = pst0 if side == 0 else pst1
                ve_v = vop(
                    lambda e, s=src, d=Vv[side][:, tb * 4:tb * 4 + 4, 0:64]:
                    e.tensor_copy(d, s))
            ps_free["t"] = ("VE", ve_v)
            if stop == "tb0" and tb == 0:
                return _finish()

        rot_ready_ve = P.cnt["VE"]
        if stop == "pair0qkv" and pair == 0:
            return _finish()

        for side in range(2):
            h = 2 * pair + side
            b0 = 64 * side
            P.wait("vector", "VE")
            vop(lambda e: e.memset(afv(f_bparts, 256), 0.0))

            for qp in range(4):
                qb_e, qb_o = 2 * qp, 2 * qp + 1
                for m_ in range(8 * qp + 8):
                    both = m_ <= 8 * qp + 3
                    key = "a0" if m_ % 2 == 0 else "b0"
                    psu = ps_a if m_ % 2 == 0 else ps_b
                    P.wait("tensor", "VE", rot_ready_ve)
                    ps_wait("tensor", key)
                    if both:
                        mm(psu[:, 0:512],
                           abv(o_rotK + m_ * 128, 128, b0, b0 + 64),
                           abv(o_rotQ + qb_e * QW, QW, b0, b0 + 64),
                           start=True, stop=True)
                    mm(psu[:, 512:1024],
                       abv(o_rotK + m_ * 128, 128, b0, b0 + 64),
                       abv(o_rotQ + qb_o * QW, QW, b0, b0 + 64),
                       start=True, stop=True, sem="TE")
                    te_s = P.cnt["TE"]
                    P.wait("scalar", "TE", te_s)
                    P.wait("scalar", "VE", max(ve_p_m[m_], bias_ready_ve))
                    bias_ap = afv(f_bias + m_, 1)
                    if both:
                        se_v = act(abv(o_P + m_ * 1024, 1024), psu[:],
                                   AF.Exp, bias=bias_ap, scale=0.125)
                    else:
                        se_v = act(abv(o_P + m_ * 1024 + 512, 512),
                                   psu[:, 512:1024],
                                   AF.Exp, bias=bias_ap, scale=0.125)
                    ps_free[key] = ("SE", se_v)
                    ps_free["a1" if m_ % 2 == 0 else "b1"] = ("SE", se_v)
                    if 8 * qp <= m_ <= 8 * qp + 3:
                        P.wait("vector", "SE", se_v)
                        vop(lambda e, p=abv(o_P + m_ * 1024, 512),
                            dmk=s_dm[:, (m_ - 8 * qp) * QW:(m_ - 8 * qp + 1) * QW]:
                            e.tensor_mul(p, p, dmk))
                    if m_ >= 8 * qp + 4:
                        P.wait("vector", "SE", se_v)
                        vop(lambda e, p=abv(o_P + m_ * 1024 + 512, 512),
                            dmk=s_dm[:, (m_ - 8 * qp - 4) * QW:(m_ - 8 * qp - 3) * QW]:
                            e.tensor_mul(p, p, dmk))
                se_exp_done = P.cnt["SE"]
                ve_mask_done = P.cnt["VE"]

                ve_rec = {}
                for qb in (qb_e, qb_o):
                    half = 512 * (qb % 2)
                    pso, pkey = (ps_d, "d") if qb % 2 == 0 else (ps_e, "e")
                    nmt = 4 * qb + 4
                    P.wait("tensor", "SE", se_exp_done)
                    P.wait("tensor", "VE", ve_mask_done)
                    ps_wait("tensor", pkey)
                    for m_ in range(nmt):
                        mm(pso[0:65, :], Vv[side][:, m_, :],
                           abv(o_P + m_ * 1024 + half, 512),
                           start=(m_ == 0), stop=(m_ == nmt - 1),
                           sem="TE" if m_ == nmt - 1 else None)
                    te_av = P.cnt["TE"]
                    P.wait("vector", "TE", max(te_av, o65_te_user))
                    ve_o65 = vop(lambda e, p=pso, h2=half:
                                 e.tensor_copy(afv(f_o65 + h2, QW, 0, 65),
                                               p[0:65, :]))
                    ps_free[pkey] = ("VE", ve_o65)
                    P.wait("vector", "VE")
                    vop(lambda e, h2=half: e.reciprocal(
                        afv(f_o65 + h2, QW, 64, 65), afv(f_o65 + h2, QW, 64, 65)))
                    ve_rec[qb] = P.cnt["VE"]
                for qb in (qb_e, qb_o):
                    half = 512 * (qb % 2)
                    nmt = 4 * qb + 4
                    ps_wait("tensor", "c")
                    P.wait("tensor", "VE", ve_rec[qb])
                    mm(ps_c[:], s_s65[:], afv(f_o65 + half, QW, 0, 65),
                       start=True, stop=True, sem="TE")
                    o65_te_user = P.cnt["TE"]
                    P.wait("vector", "TE", o65_te_user)
                    ve_rbb = vop(lambda e: e.tensor_copy(abv(o_rbb, QW), ps_c[:]))
                    ps_free["c"] = ("VE", ve_rbb)
                    P.wait("vector", "VE")
                    P.wait("vector", "QA", attn_store[side])
                    vop(lambda e, d=abv(o_attn[side] + qb * QW, QW, 0, 64), h2=half:
                        e.tensor_mul(d, afv(f_o65 + h2, QW, 0, 64),
                                     abv(o_rbb, QW, 0, 64)))
                    for m_ in range(nmt):
                        ve_p_m[m_] = vop(
                            lambda e, p=abv(o_P + m_ * 1024 + half, 512),
                            acc=afv(f_bparts + m_ * 8 + qb, 1):
                            e.scalar_tensor_tensor(
                                abv(o_bps, QW), p, 1.0, abv(o_rbb, QW),
                                ALU.mult, ALU.mult, accum_out=acc))

            P.wait("scalar", "VE", P.cnt["VE"])
            dma("scalar", d_attnT[h * 64:(h + 1) * 64, :],
                abv(o_attn[side], NT, 0, 64), "QA")
            attn_store[side] = P.cnt["QA"]

            if h < NH - 1:
                P.wait("vector", "VE")
                for m_ in range(NST):
                    vop(lambda e, m=m_: e.reduce_sum(
                        afv(f_bsums + m, 1), afv(f_bparts + m * 8, 8),
                        axis=AX.X))
                P.wait("vector", "VE")
                P.wait("vector", "SE", P.cnt["SE"])
                vop(lambda e, hc=h: e.tensor_scalar(
                    afv(f_bias, 32), afv(f_bsums, 32),
                    s_cascT[:, hc:hc + 1], SHIFT, ALU.mult, ALU.add))
                bias_ready_ve = P.cnt["VE"]
            if stop == "head0" and h == 0:
                return _finish()
        if stop == "pair0" and pair == 0:
            return _finish()
        pair_end_te = P.cnt["TE"]

    v_attn_qa = P.cnt["QA"]
    if stop == "attn":
        return _finish()
    P.barrier()
    ab.pop()
    af.pop()

    ab.push()
    o_x2T = ab.alloc(8 * NT)
    o_acc = ab.alloc(8 * NT)
    o_maskT = ab.alloc(NT)

    ab.push()
    af.push()
    o_wpj = ab.alloc(8 * 1024)
    o_atc = [ab.alloc(8 * TB) for _ in range(2)]
    o_pj = [ab.alloc(TB) for _ in range(2)]

    P.wait("sync", "QG", v_wpb)
    dma("sync", abv(o_wpj, 8 * 1024),
        d_wpb.rearrange("(cc p) o -> p cc o", p=128), "QS")
    v_wpj = P.cnt["QS"]

    atc_free = [0, 0]
    pj_free = [0, 0]
    for tb in range(NTB):
        sl = tb % 2
        P.wait("sync", "QA", v_attn_qa)
        P.wait("sync", "TE", atc_free[sl])
        for cc in range(8):
            dma("sync", abv(o_atc[sl] + cc * TB, TB),
                d_attnT[cc * 128:(cc + 1) * 128, tb * TB:(tb + 1) * TB], "QS")
        v_atc = P.cnt["QS"]
        P.wait("tensor", "QS", v_wpj)
        P.wait("tensor", "QS", v_atc)
        for oc in range(8):
            half = oc % 2
            key = "a%d" % half
            pv = ps_a[:, half * 512:(half + 1) * 512]
            ps_wait("tensor", key)
            for cc in range(8):
                mm(pv, abv(o_wpj + cc * 1024 + oc * 128, 128),
                   abv(o_atc[sl] + cc * TB, TB),
                   start=(cc == 0), stop=(cc == 7),
                   sem="TE" if cc == 7 else None)
            te_pj = P.cnt["TE"]
            psl = oc % 2
            P.wait("vector", "TE", te_pj)
            P.wait("vector", "QA", pj_free[psl])
            ve_pj = vop(lambda e, d=abv(o_pj[psl], TB), p=pv: e.tensor_copy(d, p))
            ps_free[key] = ("VE", ve_pj)
            P.wait("scalar", "VE", ve_pj)
            dma("scalar", d_projT[oc * 128:(oc + 1) * 128, tb * TB:(tb + 1) * TB],
                abv(o_pj[psl], TB), "QA")
            pj_free[psl] = P.cnt["QA"]
        atc_free[sl] = P.cnt["TE"]
    v_projT_qa = P.cnt["QA"]
    if stop == "proj":
        return _finish()
    P.barrier()
    ab.pop()
    af.pop()

    ab.push()
    af.push()
    o_pn = [ab.alloc(1024) for _ in range(2)]
    o_x2b = [ab.alloc(1024) for _ in range(2)]
    o_oh = ab.alloc(16)
    f_xr = [af.alloc(1024) for _ in range(2)]
    f_y = af.alloc(1024)
    f_sq = af.alloc(1024)
    f_x2f = af.alloc(1024)
    f_x2Tf = af.alloc(1024)
    f_cols = af.alloc(16)
    f_sc = af.alloc(16)

    pn_free = [0, 0]
    xr_free = [0, 0]
    x2b_free = [0, 0]
    se_stats = 0
    x2n_store = 0
    for tt in range(32):
        sl = tt % 2
        P.wait("sync", "QA", v_projT_qa)
        P.wait("sync", "VE", pn_free[sl])
        dma("sync", abv(o_pn[sl], 1024),
            d_projT[0:1024, tt * 128:(tt + 1) * 128], "QS", transpose=True)
        v_pn = P.cnt["QS"]
        P.wait("scalar", "VE", xr_free[sl])
        dma("scalar", afv(f_xr[sl], 1024), d_x[tt * 128:(tt + 1) * 128, :], "QA")
        v_xr = P.cnt["QA"]

        P.wait("vector", "QS", v_pn)
        P.wait("vector", "QA", v_xr)
        P.wait("vector", "SE", se_stats)
        ve_y = vop(lambda e, s=sl: e.tensor_add(afv(f_y, 1024), afv(f_xr[s], 1024),
                                                abv(o_pn[s], 1024)))
        pn_free[sl] = ve_y
        xr_free[sl] = ve_y
        P.wait("scalar", "VE", ve_y)
        act(afv(f_sq, 1024), afv(f_y, 1024), AF.Square, accum=afv(f_cols + 1, 1))
        se_stats = act(afv(f_sq, 1024), afv(f_y, 1024), AF.Copy,
                       accum=afv(f_cols + 0, 1))
        P.wait("vector", "SE", se_stats)
        vop(lambda e: e.tensor_scalar_mul(afv(f_cols + 2, 1), afv(f_cols + 0, 1),
                                          1.0 / C))
        P.wait("vector", "VE")
        vop(lambda e: e.tensor_mul(afv(f_cols + 3, 1), afv(f_cols + 2, 1),
                                   afv(f_cols + 2, 1)))
        vop(lambda e: e.tensor_scalar_mul(afv(f_cols + 4, 1), afv(f_cols + 1, 1),
                                          1.0 / C))
        P.wait("vector", "VE")
        vop(lambda e: e.scalar_tensor_tensor(afv(f_cols + 5, 1), afv(f_cols + 4, 1),
                                             EPS, afv(f_cols + 3, 1),
                                             ALU.add, ALU.subtract))
        ve_var = P.cnt["VE"]
        P.wait("scalar", "VE", ve_var)
        se_sqv = act(afv(f_cols + 6, 1), afv(f_cols + 5, 1), AF.Sqrt)
        P.wait("vector", "SE", se_sqv)
        vop(lambda e: e.reciprocal(afv(f_cols + 7, 1), afv(f_cols + 6, 1)))
        P.wait("vector", "VE")
        P.wait("vector", "QS", x2b_free[sl])
        vop(lambda e, s=sl: e.tensor_scalar(abv(o_x2b[s], 1024), afv(f_y, 1024),
                                            afv(f_cols + 2, 1), afv(f_cols + 7, 1),
                                            ALU.subtract, ALU.mult))
        vop(lambda e: e.tensor_scalar(afv(f_x2f, 1024), afv(f_y, 1024),
                                      afv(f_cols + 2, 1), afv(f_cols + 7, 1),
                                      ALU.subtract, ALU.mult))
        ve_x2 = P.cnt["VE"]
        P.wait("sync", "VE", ve_x2)
        dma("sync", d_x2n[tt * 128:(tt + 1) * 128, :], abv(o_x2b[sl], 1024), "QS")
        x2b_free[sl] = P.cnt["QS"]
        x2n_store = P.cnt["QS"]

        ps_wait("tensor", "c", "d")
        P.wait("tensor", "VE", ve_x2)
        for cc in range(8):
            pq = (ps_c if cc < 4 else ps_d)[:, (cc % 4) * 128:(cc % 4 + 1) * 128]
            P.emit("tensor", lambda e, o=pq, i=afv(f_x2f + cc * 128, 128),
                   idn=s_if[:]: e.transpose(o, i, idn), "TE")
            P.wait("tensor", "TE")
        te_tp = P.cnt["TE"]
        P.wait("vector", "TE", te_tp)
        vop(lambda e: e.tensor_copy(afv(f_x2Tf, 512), ps_c[:]))
        ve_cd = vop(lambda e: e.tensor_copy(afv(f_x2Tf + 512, 512), ps_d[:]))
        ps_free["c"] = ("VE", ve_cd)
        ps_free["d"] = ("VE", ve_cd)
        ps_wait("tensor", "e")
        P.wait("tensor", "VE", ve_cd)
        for cc in range(8):
            mm(ps_e[:, 0:8], afv(f_x2Tf + cc * 128, 128), s_wr[:, cc * 8:cc * 8 + 8],
               start=(cc == 0), stop=False)
        mm(ps_e[:, 0:8], s_1r[:], s_br[:], start=False, stop=True, sem="TE")
        te_sc = P.cnt["TE"]
        P.wait("vector", "TE", te_sc)
        ve_sc = vop(lambda e: e.tensor_copy(afv(f_sc, 8), ps_e[:, 0:8]))
        ps_free["e"] = ("VE", ve_sc)
        P.wait("vector", "VE")
        vop(lambda e: e.tensor_max(afv(f_sc + 8, 4), afv(f_sc, 4),
                                   afv(f_sc + 4, 4)))
        P.wait("vector", "VE")
        vop(lambda e: e.tensor_max(afv(f_sc + 12, 2), afv(f_sc + 8, 2),
                                   afv(f_sc + 10, 2)))
        P.wait("vector", "VE")
        vop(lambda e: e.tensor_max(afv(f_sc + 14, 1), afv(f_sc + 12, 1),
                                   afv(f_sc + 13, 1)))
        P.wait("vector", "VE")
        vop(lambda e: e.tensor_single_scalar(abv(o_oh, 8), afv(f_sc, 8),
                                             afv(f_sc + 14, 1), ALU.is_ge))
        ve_oh = P.cnt["VE"]
        ps_wait("tensor", "t")
        P.wait("tensor", "VE", ve_oh)
        P.emit("tensor", lambda e: e.transpose(ps_t[0:8, 0:128], abv(o_oh, 8),
                                               s_ib[:]), "TE")
        te_mt = P.cnt["TE"]
        P.wait("vector", "TE", te_mt)
        ve_mt = vop(lambda e, t=tt: e.tensor_copy(
            abv(o_maskT + t * 128, 128, 0, 8), ps_t[0:8, 0:128]))
        ps_free["t"] = ("VE", ve_mt)
    ve_maskT = P.cnt["VE"]

    P.wait("sync", "QS", x2n_store)
    for cc in range(8):
        dma("sync", abv(o_x2T + cc * NT, NT),
            d_x2n[0:NT, cc * 128:(cc + 1) * 128], "QS", transpose=True)
    v_x2T = P.cnt["QS"]
    if "msk" in dbg:
        P.wait("sync", "VE", ve_maskT)
        dma("sync", d_msk[:], abv(o_maskT, NT, 0, 8), "QS")

    if stop == "ln1":
        return _finish()
    P.barrier()
    ab.pop()
    af.pop()

    ab.push()
    af.push()
    o_h = ab.alloc(16 * TB)
    o_w1r = [ab.alloc(1024) for _ in range(4)]
    o_w2r = [ab.alloc(2048) for _ in range(2)]
    o_mbc = [ab.alloc(TB) for _ in range(2)]
    o_ftmp = ab.alloc(TB)
    f_b1 = af.alloc(32)
    f_b2 = af.alloc(8)

    w1r_free = [0] * 4
    w2r_free = [0] * 2
    mbc_free = [0, 0]
    h_free_te = 0
    b1_free_se = 0
    b2_free_ve = 0

    for tb in range(NTB):
        for e_ in range(NE):
            esl = (tb * NE + e_) % 2
            P.wait("scalar", "SE", b1_free_se)
            P.wait("scalar", "VE", b2_free_ve)
            dma("scalar", afv(f_b1, 32), d_b1[e_], "QA")
            dma("scalar", afv(f_b2, 8), d_b2[e_], "QA")
            v_b12 = P.cnt["QA"]
            P.wait("tensor", "VE", ve_maskT)
            ps_wait("tensor", "c")
            mm(ps_c[:], s_s8[:, e_ * 128:(e_ + 1) * 128],
               abv(o_maskT + tb * TB, TB, 0, 8), start=True, stop=True,
               sem="TE")
            te_mbc = P.cnt["TE"]
            P.wait("vector", "TE", te_mbc)
            P.wait("vector", "VE", mbc_free[esl])
            ve_mbc = vop(lambda e, d=abv(o_mbc[esl], TB): e.tensor_copy(d, ps_c[:]))
            ps_free["c"] = ("VE", ve_mbc)

            for half in range(2):
                for ft in range(16):
                    wsl = ft % 4
                    P.wait("sync", "QG", v_w1b[e_])
                    P.wait("sync", "TE", w1r_free[wsl])
                    src = d_w1b[e_].rearrange("(cc p) f -> p cc f", p=128)[
                        :, :, half * 2048 + ft * 128: half * 2048 + (ft + 1) * 128]
                    dma("sync", abv(o_w1r[wsl], 1024), src, "QS")
                    v_w1 = P.cnt["QS"]
                    ph = ft % 2
                    key = "a%d" % ph
                    pv = ps_a[:, ph * 512:(ph + 1) * 512]
                    P.wait("tensor", "QS", max(v_w1, v_x2T))
                    ps_wait("tensor", key)
                    for cc in range(8):
                        mm(pv, abv(o_w1r[wsl] + cc * 128, 128),
                           abv(o_x2T + cc * NT + tb * TB, TB),
                           start=(cc == 0), stop=(cc == 7),
                           sem="TE" if cc == 7 else None)
                    te_m1 = P.cnt["TE"]
                    w1r_free[wsl] = te_m1
                    P.wait("scalar", "QA", v_b12)
                    P.wait("scalar", "TE", max(te_m1, h_free_te))
                    se_g = act(abv(o_h + ft * TB, TB), pv, AF.Gelu,
                               bias=afv(f_b1 + half * 16 + ft, 1))
                    ps_free[key] = ("SE", se_g)
                se_h = P.cnt["SE"]
                b1_free_se = se_h

                for ht in range(8):
                    wsl = ht % 2
                    P.wait("scalar", "QG", v_w2b[e_])
                    P.wait("scalar", "TE", w2r_free[wsl])
                    src = d_w2b[e_].rearrange("(fc p) hh -> p fc hh", p=128)[
                        :, half * 16:(half + 1) * 16, ht * 128:(ht + 1) * 128]
                    dma("scalar", abv(o_w2r[wsl], 2048), src, "QA")
                    v_w2 = P.cnt["QA"]
                    ph = ht % 2
                    key = "b%d" % ph
                    pv = ps_b[:, ph * 512:(ph + 1) * 512]
                    P.wait("tensor", "QA", v_w2)
                    P.wait("tensor", "SE", se_h)
                    ps_wait("tensor", key)
                    for fc in range(16):
                        mm(pv, abv(o_w2r[wsl] + fc * 128, 128),
                           abv(o_h + fc * TB, TB),
                           start=(fc == 0), stop=(fc == 15),
                           sem="TE" if fc == 15 else None)
                    te_m2 = P.cnt["TE"]
                    w2r_free[wsl] = te_m2
                    if ht == 7:
                        h_free_te = te_m2
                    P.wait("vector", "TE", te_m2)
                    P.wait("vector", "VE", ve_mbc)
                    accs = abv(o_acc + ht * NT + tb * TB, TB)
                    if e_ == 0 and half == 0:
                        ve_m2 = vop(lambda e, d=accs, p=pv,
                                    m=abv(o_mbc[esl], TB): e.tensor_mul(d, p, m))
                    else:
                        if half == 0:
                            vop(lambda e, p=pv, m=abv(o_mbc[esl], TB):
                                e.tensor_mul(abv(o_ftmp, TB), p, m))
                        else:
                            vop(lambda e, p=pv, m=abv(o_mbc[esl], TB),
                                b2=afv(f_b2 + ht, 1):
                                e.scalar_tensor_tensor(
                                    abv(o_ftmp, TB), p, b2, m,
                                    ALU.add, ALU.mult))
                        P.wait("vector", "VE")
                        ve_m2 = vop(lambda e, d=accs:
                                    e.tensor_add(d, d, abv(o_ftmp, TB)))
                    ps_free[key] = ("VE", ve_m2)
            mbc_free[esl] = P.cnt["VE"]
            b2_free_ve = P.cnt["VE"]

    if stop == "ffn":
        return _finish()
    P.barrier()
    ab.pop()
    af.pop()

    ab.push()
    af.push()
    o_y2 = [ab.alloc(1024) for _ in range(2)]
    f_yn = af.alloc(1024)
    f_out = [af.alloc(1024) for _ in range(2)]
    f_sq2 = af.alloc(1024)
    f_c2 = af.alloc(16)

    for ch in range(8):
        P.wait("vector", "VE")
        vop(lambda e, a=abv(o_acc + ch * NT, NT), x=abv(o_x2T + ch * NT, NT):
            e.tensor_add(a, a, x))
        ve_add = P.cnt["VE"]
        P.wait("sync", "VE", ve_add)
        dma("sync", d_y2T[ch * 128:(ch + 1) * 128, :], abv(o_acc + ch * NT, NT),
            "QS")
    v_y2T = P.cnt["QS"]

    y2_free = [0, 0]
    out_free = [0, 0]
    se_stats2 = 0
    for tt in range(32):
        sl = tt % 2
        P.wait("sync", "QS", v_y2T)
        P.wait("sync", "SE", y2_free[sl])
        dma("sync", abv(o_y2[sl], 1024),
            d_y2T[0:1024, tt * 128:(tt + 1) * 128], "QS", transpose=True)
        v_y2 = P.cnt["QS"]
        P.wait("scalar", "QS", v_y2)
        P.wait("scalar", "ST", out_free[sl])
        P.wait("scalar", "SE", se_stats2)
        act(afv(f_sq2, 1024), abv(o_y2[sl], 1024), AF.Square,
            accum=afv(f_c2 + 1, 1))
        y2_free[sl] = act(afv(f_out[sl], 1024), abv(o_y2[sl], 1024), AF.Copy,
                          accum=afv(f_c2 + 0, 1))
        se_stats2 = y2_free[sl]
        P.wait("vector", "SE", se_stats2)
        vop(lambda e: e.tensor_scalar_mul(afv(f_c2 + 2, 1), afv(f_c2 + 0, 1),
                                          1.0 / C))
        P.wait("vector", "VE")
        vop(lambda e: e.tensor_mul(afv(f_c2 + 3, 1), afv(f_c2 + 2, 1),
                                   afv(f_c2 + 2, 1)))
        vop(lambda e: e.tensor_scalar_mul(afv(f_c2 + 4, 1), afv(f_c2 + 1, 1),
                                          1.0 / C))
        P.wait("vector", "VE")
        vop(lambda e: e.scalar_tensor_tensor(afv(f_c2 + 5, 1), afv(f_c2 + 4, 1),
                                             EPS, afv(f_c2 + 3, 1),
                                             ALU.add, ALU.subtract))
        ve_v2 = P.cnt["VE"]
        P.wait("scalar", "VE", ve_v2)
        se_sq2 = act(afv(f_c2 + 6, 1), afv(f_c2 + 5, 1), AF.Sqrt)
        P.wait("vector", "SE", se_sq2)
        vop(lambda e: e.reciprocal(afv(f_c2 + 7, 1), afv(f_c2 + 6, 1)))
        P.wait("vector", "VE")
        ve_out = vop(lambda e, s=sl: e.tensor_scalar(
            afv(f_out[s], 1024), afv(f_out[s], 1024),
            afv(f_c2 + 2, 1), afv(f_c2 + 7, 1), ALU.subtract, ALU.mult))
        P.wait("sync", "VE", ve_out)
        dma("sync", d_out[tt * 128:(tt + 1) * 128, :], afv(f_out[sl], 1024), "ST")
        out_free[sl] = P.cnt["ST"]

    P.wait("sync", "ST", P.cnt["ST"])

    with nc.Block() as block:
        for eng in ENGS:
            dec = getattr(block, eng)
            lst = P.ops[eng]

            def body(e, lst=lst):
                for fn in lst:
                    fn(e)
            dec(body)

    es.close()
    return nc


def kernel(L0_attn_x, L0_attn_Wqkv, L0_attn_Wproj, L0_attn_cascade_w,
           L0_ffn_wr, L0_ffn_br, L0_ffn_w1, L0_ffn_b1, L0_ffn_w2, L0_ffn_b2):
    x = np.ascontiguousarray(np.asarray(L0_attn_x, dtype=np.float32))
    w = _arrange_weights(np.asarray(L0_attn_Wqkv, np.float32),
                         np.asarray(L0_attn_Wproj, np.float32),
                         np.asarray(L0_attn_cascade_w, np.float32),
                         np.asarray(L0_ffn_wr, np.float32),
                         np.asarray(L0_ffn_br, np.float32),
                         np.asarray(L0_ffn_w1, np.float32),
                         np.asarray(L0_ffn_b1, np.float32),
                         np.asarray(L0_ffn_w2, np.float32),
                         np.asarray(L0_ffn_b2, np.float32))
    cst = _host_constants()
    if "nc" not in _CACHE:
        _CACHE["nc"] = build()
    nc = _CACHE["nc"]
    shared = dict(wqk=w["wqk"], wv=w["wv"], wproj=w["wproj"], casc=w["casc"],
                  wr_arr=w["wr_arr"], br_arr=w["br_arr"], w1=w["w1"],
                  b1_arr=w["b1_arr"], w2=w["w2"], b2_arr=w["b2_arr"],
                  rope_a=cst["rope_a"], rope_b=cst["rope_b"],
                  dmask=cst["dmask"], sel65=cst["sel65"], sel8=cst["sel8"],
                  iden2=cst["iden2"], idenb=cst["idenb"], idenf=cst["idenf"],
                  ones_r=cst["ones_r"])
    in_maps = []
    for b in range(8):
        m = dict(shared)
        m["x"] = np.ascontiguousarray(x[b])
        in_maps.append(m)
    res = run_bass_kernel_spmd(nc, in_maps, core_ids=list(range(8)))
    out = np.stack([np.asarray(r["out"]) for r in res.results], axis=0)
    return out.astype(np.float32)


# version 05
# speedup vs baseline: 1.0580x; 1.0580x over previous
from contextlib import ExitStack

import numpy as np
import ml_dtypes

import concourse.bass as bass
import concourse.mybir as mybir
from concourse.bass_utils import run_bass_kernel_spmd

F32 = mybir.dt.float32
BF16 = mybir.dt.bfloat16
AF = mybir.ActivationFunctionType
ALU = mybir.AluOpType
AX = mybir.AxisListType

NT = 4096
C = 1024
NH = 16
HD = 64
NPAIR = 8
NTB = 8
TB = 512
NST = 32
QW = 512
NE = 8
FF = 4096
SHIFT = -16.0
EPS = 1e-5

BF = ml_dtypes.bfloat16

_CACHE = {}


_TABLE_PROG = r"""
import numpy as np, sys
import jax.numpy as jnp
NT, NH, HD = 4096, 16, 64
inv_freq = 1.0 / (10000.0 ** (jnp.arange(0, HD, 2, dtype=jnp.float32) / HD))
t = jnp.arange(NT, dtype=jnp.float32)
cos_t, sin_t = [], []
for i in range(NH):
    freq_scale = 10000.0 ** (i / NH)
    freqs = jnp.outer(t, inv_freq * freq_scale)
    cos_t.append(np.asarray(jnp.cos(freqs)).T.astype(np.float32))
    sin_t.append(np.asarray(jnp.sin(freqs)).T.astype(np.float32))
np.stack([np.stack(cos_t), np.stack(sin_t)]).tofile(sys.argv[1])
"""


def _rope_tables_cpu():
    import os, subprocess, sys, tempfile
    fd, path = tempfile.mkstemp(suffix=".bin")
    os.close(fd)
    try:
        env = {**os.environ, "JAX_PLATFORMS": "cpu", "PYTHONPATH": ""}
        subprocess.run([sys.executable, "-c", _TABLE_PROG, path],
                       env=env, check=True, capture_output=True, timeout=600)
        z = np.fromfile(path, dtype=np.float32).reshape(2, NH, 32, NT)
        return [z[0][i] for i in range(NH)], [z[1][i] for i in range(NH)]
    finally:
        os.unlink(path)


def _host_constants():
    if "const" in _CACHE:
        return _CACHE["const"]
    cos_t, sin_t = _rope_tables_cpu()
    rope_a = np.zeros((NPAIR, 128, NT), dtype=np.float32)
    rope_b = np.zeros((NPAIR, 128, NT), dtype=np.float32)
    for p in range(NPAIR):
        h0, h1 = 2 * p, 2 * p + 1
        rope_a[p, 0:32] = cos_t[h0]
        rope_a[p, 32:64] = cos_t[h0]
        rope_a[p, 64:96] = cos_t[h1]
        rope_a[p, 96:128] = cos_t[h1]
        rope_b[p, 0:32] = -sin_t[h0]
        rope_b[p, 32:64] = sin_t[h0]
        rope_b[p, 64:96] = -sin_t[h1]
        rope_b[p, 96:128] = sin_t[h1]

    dmask = np.zeros((128, 4 * QW), dtype=np.float32)
    for mp in range(4):
        for s in range(128):
            q0 = mp * 128 + s
            if q0 < QW:
                dmask[s, mp * QW + q0:(mp + 1) * QW] = 1.0
    sel65 = np.zeros((65, 128), dtype=np.float32)
    sel65[64, :] = 1.0
    sel8 = np.zeros((8, 8 * 128), dtype=np.float32)
    for e in range(8):
        sel8[e, e * 128:(e + 1) * 128] = 1.0
    iden2 = np.zeros((128, 64), dtype=np.float32)
    iden2[0:64] = np.eye(64, dtype=np.float32)
    iden2[64:128] = np.eye(64, dtype=np.float32)
    idenb = np.eye(128, dtype=np.float32)
    c = dict(rope_a=rope_a, rope_b=rope_b, dmask=dmask,
             sel65=sel65, sel8=sel8, iden2=iden2,
             idenb=idenb, idenf=idenb.astype(np.float32),
             ones_r=np.ones((1, 128), dtype=np.float32))
    _CACHE["const"] = c
    return c


def _arrange_weights(Wqkv, Wproj, cascade_w, wr, br, w1, b1, w2, b2):
    ev = np.arange(0, HD, 2)
    od = np.arange(1, HD, 2)
    tiles = []
    for p in range(NPAIR):
        h0, h1 = 2 * p, 2 * p + 1
        q0, q1 = h0 * HD, h1 * HD
        k0, k1 = C + h0 * HD, C + h1 * HD
        kx = np.concatenate([Wqkv[:, k0 + ev], Wqkv[:, k0 + od],
                             Wqkv[:, k1 + ev], Wqkv[:, k1 + od]], axis=1)
        ky = np.concatenate([Wqkv[:, k0 + od], Wqkv[:, k0 + ev],
                             Wqkv[:, k1 + od], Wqkv[:, k1 + ev]], axis=1)
        qx = np.concatenate([Wqkv[:, q0 + ev], Wqkv[:, q0 + od],
                             Wqkv[:, q1 + ev], Wqkv[:, q1 + od]], axis=1)
        qy = np.concatenate([Wqkv[:, q0 + od], Wqkv[:, q0 + ev],
                             Wqkv[:, q1 + od], Wqkv[:, q1 + ev]], axis=1)
        tiles.append(np.concatenate([kx, ky, qx, qy], axis=1))
    return dict(
        wqk=np.ascontiguousarray(np.concatenate(tiles, axis=1)),
        wv=np.ascontiguousarray(Wqkv[:, 2 * C:3 * C]),
        wproj=np.ascontiguousarray(Wproj),
        wr_arr=np.ascontiguousarray(
            wr.reshape(8, 128, 8).transpose(1, 0, 2).reshape(128, 64)),
        br_arr=np.ascontiguousarray(br.reshape(1, 8)),
        b1_arr=np.ascontiguousarray(b1.reshape(8, 32, 128).transpose(0, 2, 1)),
        b2_arr=np.ascontiguousarray(b2.reshape(8, 8, 128).transpose(0, 2, 1)),
        casc=np.ascontiguousarray(np.broadcast_to(
            np.asarray(cascade_w).reshape(1, 15), (128, 15))),
        w1=np.ascontiguousarray(w1), w2=np.ascontiguousarray(w2))


ENGS = ("tensor", "scalar", "vector", "sync", "gpsimd")


class Plan:
    def __init__(self, sems):
        self.sems = sems
        self.cnt = {k: 0 for k in sems}
        self.ops = {e: [] for e in ENGS}
        self.last_wait = {e: {} for e in ENGS}

    def wait(self, eng, sem, val=None):
        v = self.cnt[sem] if val is None else val
        assert v <= self.cnt[sem], f"wait on future {sem}: {v} > {self.cnt[sem]}"
        if v <= 0 or self.last_wait[eng].get(sem, -1) >= v:
            return
        self.last_wait[eng][sem] = v
        h = self.sems[sem]
        self.ops[eng].append(lambda e, h=h, v=v: e.wait_ge(h, v))

    def emit(self, eng, fn, sem=None, n=1):
        if sem is None:
            self.ops[eng].append(fn)
            return None
        h = self.sems[sem]
        self.cnt[sem] += n
        self.ops[eng].append(lambda e, fn=fn, h=h, n=n: fn(e).then_inc(h, n))
        return self.cnt[sem]

    def barrier(self):
        snap = dict(self.cnt)
        for eng in ENGS:
            for s, v in snap.items():
                self.wait(eng, s, v)


class Arena:
    def __init__(self, cols):
        self.cols = cols
        self.off = 0
        self.marks = []

    def alloc(self, n):
        o = (self.off + 15) // 16 * 16
        assert o + n <= self.cols, f"arena overflow {o}+{n}>{self.cols}"
        self.off = o + n
        return o

    def push(self):
        self.marks.append(self.off)

    def pop(self):
        self.off = self.marks.pop()


def build(dbg=(), stop=None, skip=()):
    nc = bass.Bass("TRN2", target_bir_lowering=False)

    d_x = nc.dram_tensor("x", [NT, C], F32, kind="ExternalInput")
    d_wqk = nc.dram_tensor("wqk", [C, 4096], F32, kind="ExternalInput")
    d_wv = nc.dram_tensor("wv", [C, C], F32, kind="ExternalInput")
    d_wproj = nc.dram_tensor("wproj", [C, C], F32, kind="ExternalInput")
    d_casc = nc.dram_tensor("casc", [128, 15], F32, kind="ExternalInput")
    d_wr = nc.dram_tensor("wr_arr", [128, 64], F32, kind="ExternalInput")
    d_br = nc.dram_tensor("br_arr", [1, 8], F32, kind="ExternalInput")
    d_w1 = nc.dram_tensor("w1", [NE, C, FF], F32, kind="ExternalInput")
    d_b1 = nc.dram_tensor("b1_arr", [NE, 128, 32], F32, kind="ExternalInput")
    d_w2 = nc.dram_tensor("w2", [NE, FF, C], F32, kind="ExternalInput")
    d_b2 = nc.dram_tensor("b2_arr", [NE, 128, 8], F32, kind="ExternalInput")
    d_ra = nc.dram_tensor("rope_a", [NPAIR, 128, NT], F32, kind="ExternalInput")
    d_rbt = nc.dram_tensor("rope_b", [NPAIR, 128, NT], F32, kind="ExternalInput")
    d_dm = nc.dram_tensor("dmask", [128, 4 * QW], F32, kind="ExternalInput")
    d_s65 = nc.dram_tensor("sel65", [65, 128], F32, kind="ExternalInput")
    d_s8 = nc.dram_tensor("sel8", [8, 8 * 128], F32, kind="ExternalInput")
    d_i2 = nc.dram_tensor("iden2", [128, 64], F32, kind="ExternalInput")
    d_ib = nc.dram_tensor("idenb", [128, 128], F32, kind="ExternalInput")
    d_if = nc.dram_tensor("idenf", [128, 128], F32, kind="ExternalInput")
    d_1r = nc.dram_tensor("ones_r", [1, 128], F32, kind="ExternalInput")
    d_out = nc.dram_tensor("out", [NT, C], F32, kind="ExternalOutput")

    def scratch(name, shape, dtype):
        kind = "ExternalOutput" if name in dbg else "Internal"
        return nc.dram_tensor(name, shape, dtype, kind=kind)

    d_xb = scratch("xbf", [NT, C], BF16)
    d_wqkb = scratch("wqkb", [C, 4096], BF16)
    d_wvb = scratch("wvb", [C, C], BF16)
    d_wpb = scratch("wpb", [C, C], BF16)
    d_w1b = scratch("w1b", [NE, C, FF], BF16)
    d_w2b = scratch("w2b", [NE, FF, C], BF16)
    d_attnT = scratch("attnT", [C, NT], BF16)
    d_projT = scratch("projT", [C, NT], BF16)
    d_x2n = scratch("x2n", [NT, C], BF16)
    d_y2T = scratch("y2T", [C, NT], BF16)
    d_msk = scratch("msk", [8, NT], BF16)

    es = ExitStack()
    E = es.enter_context

    AB_COLS = 88576
    AF_COLS = 6400
    ab_t = E(nc.sbuf_tensor("arena_bf", [128, AB_COLS], BF16))
    af_t = E(nc.sbuf_tensor("arena_f32", [128, AF_COLS], F32))
    ab = Arena(AB_COLS)
    af = Arena(AF_COLS)

    s_dm = E(nc.sbuf_tensor("s_dm", [128, 4 * QW], BF16))
    s_s65 = E(nc.sbuf_tensor("s_s65", [65, 128], F32))
    s_s8 = E(nc.sbuf_tensor("s_s8", [8, 8 * 128], BF16))
    s_i2 = E(nc.sbuf_tensor("s_i2", [128, 64], BF16))
    s_ib = E(nc.sbuf_tensor("s_ib", [128, 128], BF16))
    s_if = E(nc.sbuf_tensor("s_if", [128, 128], F32))
    s_1r = E(nc.sbuf_tensor("s_1r", [1, 128], F32))
    s_casc = E(nc.sbuf_tensor("s_casc", [128, 15], F32))
    s_cascT = E(nc.sbuf_tensor("s_cascT", [128, 15], F32))
    s_wr = E(nc.sbuf_tensor("s_wr", [128, 64], F32))
    s_br = E(nc.sbuf_tensor("s_br", [1, 8], F32))

    ps_a = E(nc.psum_tensor("ps_a", [128, 1024], F32))
    ps_b = E(nc.psum_tensor("ps_b", [128, 1024], F32))
    ps_c = E(nc.psum_tensor("ps_c", [128, 512], F32))
    ps_t = E(nc.psum_tensor("ps_t", [128, 1024], BF16))
    ps_d = E(nc.psum_tensor("ps_d", [128, 512], F32))
    ps_e = E(nc.psum_tensor("ps_e", [128, 512], F32))

    sems = {}
    for name in ("TE", "SE", "VE", "QS", "QA", "QG", "ST", "GE"):
        sems[name] = E(nc.semaphore(name))
    P = Plan(sems)

    ps_free = {k: ("VE", 0) for k in
               ("a0", "a1", "b0", "b1", "c", "d", "e", "t")}

    def ps_wait(eng, *keys):
        for k in keys:
            s, v = ps_free[k]
            P.wait(eng, s, v)

    def abv(off, n, p0=0, p1=128):
        return ab_t[p0:p1, off:off + n]

    def afv(off, n, p0=0, p1=128):
        return af_t[p0:p1, off:off + n]

    def dma(q, out, in_, sem, **kw):
        return P.emit(q, lambda e, o=out, i=in_, kw=kw:
                      e.dma_start(out=o, in_=i, **kw), sem, 16)

    def mm(out, lhsT, rhs, start, stop, sem=None):
        return P.emit("tensor",
                      lambda e, o=out, l=lhsT, r=rhs, st=start, sp=stop:
                      e.matmul(o, l, r, start=st, stop=sp), sem)

    def act(out, in_, func, bias=0.0, scale=1.0, accum=None):
        return P.emit("scalar",
                      lambda e, o=out, i=in_, f=func, b=bias, s=scale, a=accum:
                      e.activation(o, i, f, bias=b, scale=s, accum_out=a), "SE")

    def vop(fn):
        return P.emit("vector", fn, "VE")

    flat2 = lambda d: d.rearrange("a b -> (a b)")[:]


    def _finish():
        P.barrier()
        with nc.Block() as block:
            for eng in ENGS:
                dec = getattr(block, eng)
                lst = P.ops[eng]

                def body(e, lst=lst):
                    for fn in lst:
                        fn(e)
                dec(body)
        es.close()
        return nc

    for (dt_, st_) in [(d_s65, s_s65), (d_if, s_if), (d_1r, s_1r)]:
        dma("sync", st_[:], dt_[:], "QS")
    for (dt_, st_) in [(d_casc, s_casc), (d_wr, s_wr), (d_br, s_br)]:
        dma("scalar", st_[:], dt_[:], "QA")
    for (dt_, st_) in [(d_dm, s_dm), (d_s8, s_s8), (d_i2, s_i2), (d_ib, s_ib)]:
        dma("gpsimd", st_[:], dt_[:], "QG")
    v_const_qs = P.cnt["QS"]
    v_const_qa = P.cnt["QA"]
    v_const_qg = P.cnt["QG"]

    dma("gpsimd", flat2(d_xb[0:TB, :]), flat2(d_x[0:TB, :]), "QG")
    v_xb0 = P.cnt["QG"]
    dma("gpsimd", d_wqkb[:, 0:1024], d_wqk[:, 0:1024], "QG")
    dma("gpsimd", flat2(d_wvb), flat2(d_wv), "QG")
    v_wqv0 = P.cnt["QG"]
    dma("gpsimd", flat2(d_xb[TB:NT, :]), flat2(d_x[TB:NT, :]), "QG")
    v_xb = P.cnt["QG"]
    dma("gpsimd", d_wqkb[:, 1024:4096], d_wqk[:, 1024:4096], "QG")
    v_wqv = P.cnt["QG"]
    dma("gpsimd", flat2(d_wpb), flat2(d_wproj), "QG")
    v_wpb = P.cnt["QG"]
    v_w1b, v_w2b = [], []
    for e_ in range(NE):
        dma("gpsimd", flat2(d_w1b[e_]), flat2(d_w1[e_]), "QG")
        v_w1b.append(P.cnt["QG"])
        dma("gpsimd", flat2(d_w2b[e_]), flat2(d_w2[e_]), "QG")
        v_w2b.append(P.cnt["QG"])

    P.wait("vector", "QA", v_const_qa)
    vop(lambda e: e.tensor_scalar_mul(s_cascT[:], s_casc[:], 1.0 / NT))
    if stop == "phase0":
        return _finish()

    ab.push()
    af.push()
    o_P = ab.alloc(NST * 1024)
    o_rotK = ab.alloc(NT)
    o_rotQ = ab.alloc(NT)
    o_scr = [ab.alloc(5 * TB) for _ in range(2)]
    o_tmp1 = ab.alloc(TB)
    o_tmp2 = ab.alloc(TB)
    o_V = [ab.alloc(NST * 65) for _ in range(2)]
    o_attn = [ab.alloc(NT) for _ in range(2)]
    o_xc = [ab.alloc(8 * TB) for _ in range(2)]
    o_wq = [ab.alloc(5 * 1024) for _ in range(2)]
    o_rbb = ab.alloc(QW)
    o_bps = ab.alloc(QW)
    o_bps2 = ab.alloc(QW)
    f_tab = [af.alloc(2 * TB) for _ in range(2)]
    f_o65 = af.alloc(2 * QW)
    f_bparts = af.alloc(256)
    f_bsums = af.alloc(32)
    f_bias = af.alloc(32)

    Vv = [ab_t[:, o_V[s]:o_V[s] + NST * 65].rearrange("p (m v) -> p m v", v=65)
          for s in range(2)]

    P.wait("vector", "QS", v_const_qs)
    P.wait("vector", "QG", v_const_qg)
    vop(lambda e: e.memset(afv(f_bias, 32), SHIFT))
    bias_m = [P.cnt["VE"]] * NST
    for s in range(2):
        vop(lambda e, s=s: e.memset(Vv[s][:, :, 64:65], 1.0))
    bias_ready_ve = P.cnt["VE"]

    xc_free = [0, 0]
    wq_free = [0, 0]
    tab_free = [0, 0]
    scr_free_te = [0, 0]
    ve_p_m = [("VE", 0)] * NST
    attn_store = [0, 0]
    o65_te_user = 0
    pair_end_te = 0

    for pair in range(NPAIR):
        wsl = pair % 2
        P.wait("sync", "QG", v_wqv0 if pair < 2 else v_wqv)
        P.wait("sync", "TE", wq_free[wsl])
        wq_srcs = [(d_wqkb, pair * 512), (d_wqkb, pair * 512 + 128),
                   (d_wqkb, pair * 512 + 256), (d_wqkb, pair * 512 + 384),
                   (d_wvb, pair * 128)]
        for ti, (dsrc, coff) in enumerate(wq_srcs):
            src = dsrc.rearrange("(cc p) o -> p cc o", p=128)[:, :, coff:coff + 128]
            dma("sync", abv(o_wq[wsl] + ti * 1024, 1024), src, "QS")
        v_wq = P.cnt["QS"]

        for tb in range(NTB):
            sl = tb % 2
            P.wait("sync", "QG", v_xb0 if (pair == 0 and tb == 0) else v_xb)
            P.wait("sync", "TE", xc_free[sl])
            for cc in range(8):
                dma("sync", abv(o_xc[sl] + cc * TB, TB),
                    d_xb[tb * TB:(tb + 1) * TB, cc * 128:(cc + 1) * 128],
                    "QS", transpose=True)
            v_xc = P.cnt["QS"]
            P.wait("scalar", "VE", tab_free[sl])
            dma("scalar", afv(f_tab[sl], TB),
                d_ra[pair][:, tb * TB:(tb + 1) * TB], "QA")
            dma("scalar", afv(f_tab[sl] + TB, TB),
                d_rbt[pair][:, tb * TB:(tb + 1) * TB], "QA")
            v_tab = P.cnt["QA"]

            P.wait("tensor", "QS", v_wq)
            P.wait("tensor", "QS", v_xc)
            for ti in range(5) if "qkv" not in skip else []:
                half = ti % 2
                key = "a%d" % half
                pv = ps_a[:, half * 512:(half + 1) * 512]
                ps_wait("tensor", key)
                for cc in range(8):
                    mm(pv, abv(o_wq[wsl] + ti * 1024 + cc * 128, 128),
                       abv(o_xc[sl] + cc * TB, TB),
                       start=(cc == 0), stop=(cc == 7),
                       sem="TE" if cc == 7 else None)
                te_done = P.cnt["TE"]
                P.wait("scalar", "TE", te_done)
                P.wait("scalar", "TE", scr_free_te[sl])
                se_ev = act(abv(o_scr[sl] + ti * TB, TB), pv, AF.Copy)
                ps_free[key] = ("SE", se_ev)
            xc_free[sl] = P.cnt["TE"]
            if tb == NTB - 1:
                wq_free[wsl] = P.cnt["TE"]

            P.wait("vector", "QA", v_tab)
            P.wait("vector", "SE", P.cnt["SE"])
            P.wait("vector", "TE", pair_end_te)
            for kq in range(2) if "rot" not in skip else []:
                sx = o_scr[sl] + (2 * kq) * TB
                sy = o_scr[sl] + (2 * kq + 1) * TB
                dst = (o_rotK if kq == 0 else o_rotQ) + tb * TB
                P.wait("vector", "VE")
                vop(lambda e, a=sx, t=f_tab[sl]:
                    e.tensor_mul(abv(o_tmp1, TB), abv(a, TB), afv(t, TB)))
                vop(lambda e, a=sy, t=f_tab[sl] + TB:
                    e.tensor_mul(abv(o_tmp2, TB), abv(a, TB), afv(t, TB)))
                P.wait("vector", "VE")
                vop(lambda e, d=dst:
                    e.tensor_add(abv(d, TB), abv(o_tmp1, TB), abv(o_tmp2, TB)))
            tab_free[sl] = P.cnt["VE"]

            if "vtr" in skip:
                if stop == "tb0" and tb == 0:
                    return _finish()
                continue
            ps_wait("tensor", "t")
            P.wait("tensor", "SE", P.cnt["SE"])
            P.wait("tensor", "QS", v_const_qs)
            P.wait("tensor", "QG", v_const_qg)
            for side in range(2):
                for t4 in range(4):
                    slot = side * 4 + t4
                    P.emit("tensor",
                           lambda e, o=ps_t[:, slot * 64:slot * 64 + 64],
                           i=abv(o_scr[sl] + 4 * TB + t4 * 128, 128,
                                 side * 64, side * 64 + 64),
                           idn=s_i2[side * 64:side * 64 + 64, 0:64]:
                           e.transpose(o, i, idn), "TE")
                    P.wait("tensor", "TE")
            te_tr = P.cnt["TE"]
            scr_free_te[sl] = te_tr
            P.wait("vector", "TE", te_tr)
            P.wait("vector", "TE", pair_end_te)
            pst0 = ps_t[:, 0:256].rearrange("p (s x) -> p s x", x=64)
            pst1 = ps_t[:, 256:512].rearrange("p (s x) -> p s x", x=64)
            for side in range(2):
                src = pst0 if side == 0 else pst1
                ve_v = vop(
                    lambda e, s=src, d=Vv[side][:, tb * 4:tb * 4 + 4, 0:64]:
                    e.tensor_copy(d, s))
            ps_free["t"] = ("VE", ve_v)
            if stop == "tb0" and tb == 0:
                return _finish()

        rot_ready_ve = P.cnt["VE"]
        if stop == "pair0qkv" and pair == 0:
            return _finish()

        for side in range(2):
            h = 2 * pair + side
            b0 = 64 * side

            for qp in range(4):
                qb_e, qb_o = 2 * qp, 2 * qp + 1
                for m_ in range(8 * qp + 8):
                    both = m_ <= 8 * qp + 3
                    key = "a0" if m_ % 2 == 0 else "b0"
                    psu = ps_a if m_ % 2 == 0 else ps_b
                    P.wait("tensor", "VE", rot_ready_ve)
                    ps_wait("tensor", key)
                    if both:
                        mm(psu[:, 0:512],
                           abv(o_rotK + m_ * 128, 128, b0, b0 + 64),
                           abv(o_rotQ + qb_e * QW, QW, b0, b0 + 64),
                           start=True, stop=True)
                    mm(psu[:, 512:1024],
                       abv(o_rotK + m_ * 128, 128, b0, b0 + 64),
                       abv(o_rotQ + qb_o * QW, QW, b0, b0 + 64),
                       start=True, stop=True, sem="TE")
                    te_s = P.cnt["TE"]
                    P.wait("scalar", "TE", te_s)
                    P.wait("scalar", ve_p_m[m_][0], ve_p_m[m_][1])
                    P.wait("scalar", "VE", bias_m[m_])
                    bias_ap = afv(f_bias + m_, 1)
                    if both:
                        se_v = act(abv(o_P + m_ * 1024, 1024), psu[:],
                                   AF.Exp, bias=bias_ap, scale=0.125)
                    else:
                        se_v = act(abv(o_P + m_ * 1024 + 512, 512),
                                   psu[:, 512:1024],
                                   AF.Exp, bias=bias_ap, scale=0.125)
                    ps_free[key] = ("SE", se_v)
                    ps_free["a1" if m_ % 2 == 0 else "b1"] = ("SE", se_v)
                    if 8 * qp <= m_ <= 8 * qp + 3:
                        P.wait("vector", "SE", se_v)
                        vop(lambda e, p=abv(o_P + m_ * 1024, 512),
                            dmk=s_dm[:, (m_ - 8 * qp) * QW:(m_ - 8 * qp + 1) * QW]:
                            e.tensor_mul(p, p, dmk))
                    if m_ >= 8 * qp + 4:
                        P.wait("vector", "SE", se_v)
                        vop(lambda e, p=abv(o_P + m_ * 1024 + 512, 512),
                            dmk=s_dm[:, (m_ - 8 * qp - 4) * QW:(m_ - 8 * qp - 3) * QW]:
                            e.tensor_mul(p, p, dmk))
                se_exp_done = P.cnt["SE"]
                ve_mask_done = P.cnt["VE"]

                ve_rec = {}
                for qb in (qb_e, qb_o):
                    half = 512 * (qb % 2)
                    pso, pkey = (ps_d, "d") if qb % 2 == 0 else (ps_e, "e")
                    nmt = 4 * qb + 4
                    P.wait("tensor", "SE", se_exp_done)
                    P.wait("tensor", "VE", ve_mask_done)
                    ps_wait("tensor", pkey)
                    for m_ in range(nmt):
                        mm(pso[0:65, :], Vv[side][:, m_, :],
                           abv(o_P + m_ * 1024 + half, 512),
                           start=(m_ == 0), stop=(m_ == nmt - 1),
                           sem="TE" if m_ == nmt - 1 else None)
                    te_av = P.cnt["TE"]
                    P.wait("vector", "TE", max(te_av, o65_te_user))
                    ve_o65 = vop(lambda e, p=pso, h2=half:
                                 e.tensor_copy(afv(f_o65 + h2, QW, 0, 65),
                                               p[0:65, :]))
                    ps_free[pkey] = ("VE", ve_o65)
                    P.wait("vector", "VE")
                    vop(lambda e, h2=half: e.reciprocal(
                        afv(f_o65 + h2, QW, 64, 65), afv(f_o65 + h2, QW, 64, 65)))
                    ve_rec[qb] = P.cnt["VE"]
                for qb in (qb_e, qb_o):
                    half = 512 * (qb % 2)
                    nmt = 4 * qb + 4
                    ps_wait("tensor", "c")
                    P.wait("tensor", "VE", ve_rec[qb])
                    mm(ps_c[:], s_s65[:], afv(f_o65 + half, QW, 0, 65),
                       start=True, stop=True, sem="TE")
                    o65_te_user = P.cnt["TE"]
                    P.wait("vector", "TE", o65_te_user)
                    ve_rbb = vop(lambda e: e.tensor_copy(abv(o_rbb, QW), ps_c[:]))
                    ps_free["c"] = ("VE", ve_rbb)
                    P.wait("vector", "VE")
                    P.wait("vector", "QA", attn_store[side])
                    vop(lambda e, d=abv(o_attn[side] + qb * QW, QW, 0, 64), h2=half:
                        e.tensor_mul(d, afv(f_o65 + h2, QW, 0, 64),
                                     abv(o_rbb, QW, 0, 64)))
                    last_qb = (qb == 7)
                    for m_ in range(nmt):
                        ve_p_m[m_] = ("VE", vop(
                            lambda e, p=abv(o_P + m_ * 1024 + half, 512),
                            acc=afv(f_bparts + m_ * 8 + qb, 1):
                            e.scalar_tensor_tensor(
                                abv(o_bps, QW), p, 1.0, abv(o_rbb, QW),
                                ALU.mult, ALU.mult, accum_out=acc)))
                        if last_qb and h < NH - 1:
                            qbmin = max(0, (m_ - 3 + 3) // 4)
                            P.wait("vector", "VE")
                            vop(lambda e, m=m_, q0=qbmin: e.reduce_sum(
                                afv(f_bsums + m, 1),
                                afv(f_bparts + m * 8 + q0, 8 - q0), axis=AX.X))
                            P.wait("vector", "VE")
                            bias_m[m_] = vop(lambda e, m=m_, hc=h: e.tensor_scalar(
                                afv(f_bias + m, 1), afv(f_bsums + m, 1),
                                s_cascT[:, hc:hc + 1], SHIFT, ALU.mult, ALU.add))

            P.wait("scalar", "VE", P.cnt["VE"])
            dma("scalar", d_attnT[h * 64:(h + 1) * 64, :],
                abv(o_attn[side], NT, 0, 64), "QA")
            attn_store[side] = P.cnt["QA"]

            if stop == "head0" and h == 0:
                return _finish()
        if stop == "pair0" and pair == 0:
            return _finish()
        pair_end_te = P.cnt["TE"]

    v_attn_qa = P.cnt["QA"]
    if stop == "attn":
        return _finish()
    P.barrier()
    ab.pop()
    af.pop()

    ab.push()
    o_x2T = ab.alloc(8 * NT)
    o_acc = ab.alloc(8 * NT)
    o_maskT = ab.alloc(NT)

    ab.push()
    af.push()
    o_wpj = ab.alloc(8 * 1024)
    o_atc = [ab.alloc(8 * TB) for _ in range(2)]
    o_pj = [ab.alloc(TB) for _ in range(2)]

    P.wait("sync", "QG", v_wpb)
    dma("sync", abv(o_wpj, 8 * 1024),
        d_wpb.rearrange("(cc p) o -> p cc o", p=128), "QS")
    v_wpj = P.cnt["QS"]

    atc_free = [0, 0]
    pj_free = [0, 0]
    for tb in range(NTB):
        sl = tb % 2
        P.wait("sync", "QA", v_attn_qa)
        P.wait("sync", "TE", atc_free[sl])
        for cc in range(8):
            dma("sync", abv(o_atc[sl] + cc * TB, TB),
                d_attnT[cc * 128:(cc + 1) * 128, tb * TB:(tb + 1) * TB], "QS")
        v_atc = P.cnt["QS"]
        P.wait("tensor", "QS", v_wpj)
        P.wait("tensor", "QS", v_atc)
        for oc in range(8):
            half = oc % 2
            key = "a%d" % half
            pv = ps_a[:, half * 512:(half + 1) * 512]
            ps_wait("tensor", key)
            for cc in range(8):
                mm(pv, abv(o_wpj + cc * 1024 + oc * 128, 128),
                   abv(o_atc[sl] + cc * TB, TB),
                   start=(cc == 0), stop=(cc == 7),
                   sem="TE" if cc == 7 else None)
            te_pj = P.cnt["TE"]
            psl = oc % 2
            P.wait("vector", "TE", te_pj)
            P.wait("vector", "QA", pj_free[psl])
            ve_pj = vop(lambda e, d=abv(o_pj[psl], TB), p=pv: e.tensor_copy(d, p))
            ps_free[key] = ("VE", ve_pj)
            P.wait("scalar", "VE", ve_pj)
            dma("scalar", d_projT[oc * 128:(oc + 1) * 128, tb * TB:(tb + 1) * TB],
                abv(o_pj[psl], TB), "QA")
            pj_free[psl] = P.cnt["QA"]
        atc_free[sl] = P.cnt["TE"]
    v_projT_qa = P.cnt["QA"]
    if stop == "proj":
        return _finish()
    P.barrier()
    ab.pop()
    af.pop()

    ab.push()
    af.push()
    o_pn = [ab.alloc(1024) for _ in range(2)]
    o_x2b = [ab.alloc(1024) for _ in range(2)]
    o_oh = ab.alloc(16)
    f_xr = [af.alloc(1024) for _ in range(2)]
    f_x2f = af.alloc(1024)
    f_x2Tf = af.alloc(1024)
    f_y = af.alloc(1024)
    f_sq = af.alloc(1024)
    f_cols = af.alloc(16)
    f_sc = af.alloc(16)

    pn_free = [0, 0]
    xr_free = [0, 0]
    x2b_free = [0, 0]
    se_stats = 0
    x2n_store = 0
    for tt in range(32):
        sl = tt % 2
        P.wait("sync", "QA", v_projT_qa)
        P.wait("sync", "VE", pn_free[sl])
        dma("sync", abv(o_pn[sl], 1024),
            d_projT[0:1024, tt * 128:(tt + 1) * 128], "QS", transpose=True)
        v_pn = P.cnt["QS"]
        P.wait("scalar", "VE", xr_free[sl])
        dma("scalar", afv(f_xr[sl], 1024), d_x[tt * 128:(tt + 1) * 128, :], "QA")
        v_xr = P.cnt["QA"]

        P.wait("vector", "QS", v_pn)
        P.wait("vector", "QA", v_xr)
        P.wait("vector", "SE", se_stats)
        ve_y = vop(lambda e, s=sl: e.tensor_add(afv(f_y, 1024), afv(f_xr[s], 1024),
                                                abv(o_pn[s], 1024)))
        pn_free[sl] = ve_y
        xr_free[sl] = ve_y
        P.wait("scalar", "VE", ve_y)
        act(afv(f_sq, 1024), afv(f_y, 1024), AF.Square, accum=afv(f_cols + 1, 1))
        se_stats = act(afv(f_sq, 1024), afv(f_y, 1024), AF.Copy,
                       accum=afv(f_cols + 0, 1))
        P.wait("vector", "SE", se_stats)
        vop(lambda e: e.tensor_scalar_mul(afv(f_cols + 2, 1), afv(f_cols + 0, 1),
                                          1.0 / C))
        P.wait("vector", "VE")
        vop(lambda e: e.tensor_mul(afv(f_cols + 3, 1), afv(f_cols + 2, 1),
                                   afv(f_cols + 2, 1)))
        vop(lambda e: e.tensor_scalar_mul(afv(f_cols + 4, 1), afv(f_cols + 1, 1),
                                          1.0 / C))
        P.wait("vector", "VE")
        vop(lambda e: e.scalar_tensor_tensor(afv(f_cols + 5, 1), afv(f_cols + 4, 1),
                                             EPS, afv(f_cols + 3, 1),
                                             ALU.add, ALU.subtract))
        ve_var = P.cnt["VE"]
        P.wait("scalar", "VE", ve_var)
        se_sqv = act(afv(f_cols + 6, 1), afv(f_cols + 5, 1), AF.Sqrt)
        P.wait("vector", "SE", se_sqv)
        vop(lambda e: e.reciprocal(afv(f_cols + 7, 1), afv(f_cols + 6, 1)))
        P.wait("vector", "VE")
        P.wait("vector", "QS", x2b_free[sl])
        vop(lambda e, s=sl: e.tensor_scalar(abv(o_x2b[s], 1024), afv(f_y, 1024),
                                            afv(f_cols + 2, 1), afv(f_cols + 7, 1),
                                            ALU.subtract, ALU.mult))
        vop(lambda e: e.tensor_scalar(afv(f_x2f, 1024), afv(f_y, 1024),
                                      afv(f_cols + 2, 1), afv(f_cols + 7, 1),
                                      ALU.subtract, ALU.mult))
        ve_x2 = P.cnt["VE"]
        P.wait("sync", "VE", ve_x2)
        dma("sync", d_x2n[tt * 128:(tt + 1) * 128, :], abv(o_x2b[sl], 1024), "QS")
        x2b_free[sl] = P.cnt["QS"]
        x2n_store = P.cnt["QS"]

        ps_wait("tensor", "c", "d")
        P.wait("tensor", "VE", ve_x2)
        for cc in range(8):
            pq = (ps_c if cc < 4 else ps_d)[:, (cc % 4) * 128:(cc % 4 + 1) * 128]
            P.emit("tensor", lambda e, o=pq, i=afv(f_x2f + cc * 128, 128),
                   idn=s_if[:]: e.transpose(o, i, idn), "TE")
            P.wait("tensor", "TE")
        te_tp = P.cnt["TE"]
        P.wait("vector", "TE", te_tp)
        vop(lambda e: e.tensor_copy(afv(f_x2Tf, 512), ps_c[:]))
        ve_cd = vop(lambda e: e.tensor_copy(afv(f_x2Tf + 512, 512), ps_d[:]))
        ps_free["c"] = ("VE", ve_cd)
        ps_free["d"] = ("VE", ve_cd)
        ps_wait("tensor", "e")
        P.wait("tensor", "VE", ve_cd)
        for cc in range(8):
            mm(ps_e[:, 0:8], afv(f_x2Tf + cc * 128, 128), s_wr[:, cc * 8:cc * 8 + 8],
               start=(cc == 0), stop=False)
        mm(ps_e[:, 0:8], s_1r[:], s_br[:], start=False, stop=True, sem="TE")
        te_sc = P.cnt["TE"]
        P.wait("vector", "TE", te_sc)
        ve_sc = vop(lambda e: e.tensor_copy(afv(f_sc, 8), ps_e[:, 0:8]))
        ps_free["e"] = ("VE", ve_sc)
        P.wait("vector", "VE")
        vop(lambda e: e.tensor_max(afv(f_sc + 8, 4), afv(f_sc, 4),
                                   afv(f_sc + 4, 4)))
        P.wait("vector", "VE")
        vop(lambda e: e.tensor_max(afv(f_sc + 12, 2), afv(f_sc + 8, 2),
                                   afv(f_sc + 10, 2)))
        P.wait("vector", "VE")
        vop(lambda e: e.tensor_max(afv(f_sc + 14, 1), afv(f_sc + 12, 1),
                                   afv(f_sc + 13, 1)))
        P.wait("vector", "VE")
        vop(lambda e: e.tensor_single_scalar(abv(o_oh, 8), afv(f_sc, 8),
                                             afv(f_sc + 14, 1), ALU.is_ge))
        ve_oh = P.cnt["VE"]
        ps_wait("tensor", "t")
        P.wait("tensor", "VE", ve_oh)
        P.emit("tensor", lambda e: e.transpose(ps_t[0:8, 0:128], abv(o_oh, 8),
                                               s_ib[:]), "TE")
        te_mt = P.cnt["TE"]
        P.wait("vector", "TE", te_mt)
        ve_mt = vop(lambda e, t=tt: e.tensor_copy(
            abv(o_maskT + t * 128, 128, 0, 8), ps_t[0:8, 0:128]))
        ps_free["t"] = ("VE", ve_mt)
    ve_maskT = P.cnt["VE"]

    P.wait("sync", "QS", x2n_store)
    for cc in range(8):
        dma("sync", abv(o_x2T + cc * NT, NT),
            d_x2n[0:NT, cc * 128:(cc + 1) * 128], "QS", transpose=True)
    v_x2T = P.cnt["QS"]
    if "msk" in dbg:
        P.wait("sync", "VE", ve_maskT)
        dma("sync", d_msk[:], abv(o_maskT, NT, 0, 8), "QS")

    if stop == "ln1":
        return _finish()
    P.barrier()
    ab.pop()
    af.pop()

    ab.push()
    af.push()
    o_h = ab.alloc(16 * TB)
    o_w1r = [ab.alloc(1024) for _ in range(3)]
    o_w2r = [ab.alloc(2048) for _ in range(3)]
    o_mbc = [ab.alloc(TB) for _ in range(2)]
    o_ftmp = ab.alloc(TB)
    f_b1 = af.alloc(32)
    f_b2 = af.alloc(8)

    w1r_free = [0] * 3
    w2r_free = [0] * 3
    mbc_free = [0, 0]
    h_free_te = 0
    b1_free_se = 0
    b2_free_ve = 0

    for tb in range(NTB):
        for e_ in range(NE):
            esl = (tb * NE + e_) % 2
            P.wait("scalar", "SE", b1_free_se)
            P.wait("scalar", "VE", b2_free_ve)
            dma("scalar", afv(f_b1, 32), d_b1[e_], "QA")
            dma("scalar", afv(f_b2, 8), d_b2[e_], "QA")
            v_b12 = P.cnt["QA"]
            P.wait("tensor", "VE", ve_maskT)
            ps_wait("tensor", "c")
            mm(ps_c[:], s_s8[:, e_ * 128:(e_ + 1) * 128],
               abv(o_maskT + tb * TB, TB, 0, 8), start=True, stop=True,
               sem="TE")
            te_mbc = P.cnt["TE"]
            P.wait("vector", "TE", te_mbc)
            P.wait("vector", "VE", mbc_free[esl])
            ve_mbc = vop(lambda e, d=abv(o_mbc[esl], TB): e.tensor_copy(d, ps_c[:]))
            ps_free["c"] = ("VE", ve_mbc)

            for half in range(2):
                for ft in range(16):
                    wsl = ft % 3
                    P.wait("sync", "QG", v_w1b[e_])
                    P.wait("sync", "TE", w1r_free[wsl])
                    src = d_w1b[e_].rearrange("(cc p) f -> p cc f", p=128)[
                        :, :, half * 2048 + ft * 128: half * 2048 + (ft + 1) * 128]
                    dma("sync", abv(o_w1r[wsl], 1024), src, "QS")
                    v_w1 = P.cnt["QS"]
                    ph = ft % 2
                    key = "a%d" % ph
                    pv = ps_a[:, ph * 512:(ph + 1) * 512]
                    P.wait("tensor", "QS", max(v_w1, v_x2T))
                    ps_wait("tensor", key)
                    for cc in range(8):
                        mm(pv, abv(o_w1r[wsl] + cc * 128, 128),
                           abv(o_x2T + cc * NT + tb * TB, TB),
                           start=(cc == 0), stop=(cc == 7),
                           sem="TE" if cc == 7 else None)
                    te_m1 = P.cnt["TE"]
                    w1r_free[wsl] = te_m1
                    P.wait("scalar", "QA", v_b12)
                    P.wait("scalar", "TE", max(te_m1, h_free_te))
                    se_g = act(abv(o_h + ft * TB, TB), pv, AF.Gelu,
                               bias=afv(f_b1 + half * 16 + ft, 1))
                    ps_free[key] = ("SE", se_g)
                se_h = P.cnt["SE"]
                b1_free_se = se_h

                for ht in range(8):
                    wsl = ht % 3
                    P.wait("scalar", "QG", v_w2b[e_])
                    P.wait("scalar", "TE", w2r_free[wsl])
                    src = d_w2b[e_].rearrange("(fc p) hh -> p fc hh", p=128)[
                        :, half * 16:(half + 1) * 16, ht * 128:(ht + 1) * 128]
                    dma("scalar", abv(o_w2r[wsl], 2048), src, "QA")
                    v_w2 = P.cnt["QA"]
                    ph = ht % 2
                    key = "b%d" % ph
                    pv = ps_b[:, ph * 512:(ph + 1) * 512]
                    P.wait("tensor", "QA", v_w2)
                    P.wait("tensor", "SE", se_h)
                    ps_wait("tensor", key)
                    for fc in range(16):
                        mm(pv, abv(o_w2r[wsl] + fc * 128, 128),
                           abv(o_h + fc * TB, TB),
                           start=(fc == 0), stop=(fc == 15),
                           sem="TE" if fc == 15 else None)
                    te_m2 = P.cnt["TE"]
                    w2r_free[wsl] = te_m2
                    if ht == 7:
                        h_free_te = te_m2
                    P.wait("vector", "TE", te_m2)
                    P.wait("vector", "VE", ve_mbc)
                    accs = abv(o_acc + ht * NT + tb * TB, TB)
                    if e_ == 0 and half == 0:
                        ve_m2 = vop(lambda e, d=accs, p=pv,
                                    m=abv(o_mbc[esl], TB): e.tensor_mul(d, p, m))
                    else:
                        if half == 0:
                            vop(lambda e, p=pv, m=abv(o_mbc[esl], TB):
                                e.tensor_mul(abv(o_ftmp, TB), p, m))
                        else:
                            vop(lambda e, p=pv, m=abv(o_mbc[esl], TB),
                                b2=afv(f_b2 + ht, 1):
                                e.scalar_tensor_tensor(
                                    abv(o_ftmp, TB), p, b2, m,
                                    ALU.add, ALU.mult))
                        P.wait("vector", "VE")
                        ve_m2 = vop(lambda e, d=accs:
                                    e.tensor_add(d, d, abv(o_ftmp, TB)))
                    ps_free[key] = ("VE", ve_m2)
            mbc_free[esl] = P.cnt["VE"]
            b2_free_ve = P.cnt["VE"]

    if stop == "ffn":
        return _finish()
    P.barrier()
    ab.pop()
    af.pop()

    ab.push()
    af.push()
    o_y2 = [ab.alloc(1024) for _ in range(2)]
    f_yn = af.alloc(1024)
    f_out = [af.alloc(1024) for _ in range(2)]
    f_sq2 = af.alloc(1024)
    f_c2 = af.alloc(16)

    for ch in range(8):
        P.wait("vector", "VE")
        vop(lambda e, a=abv(o_acc + ch * NT, NT), x=abv(o_x2T + ch * NT, NT):
            e.tensor_add(a, a, x))
        ve_add = P.cnt["VE"]
        P.wait("sync", "VE", ve_add)
        dma("sync", d_y2T[ch * 128:(ch + 1) * 128, :], abv(o_acc + ch * NT, NT),
            "QS")
    v_y2T = P.cnt["QS"]

    y2_free = [0, 0]
    out_free = [0, 0]
    se_stats2 = 0
    for tt in range(32):
        sl = tt % 2
        P.wait("sync", "QS", v_y2T)
        P.wait("sync", "SE", y2_free[sl])
        dma("sync", abv(o_y2[sl], 1024),
            d_y2T[0:1024, tt * 128:(tt + 1) * 128], "QS", transpose=True)
        v_y2 = P.cnt["QS"]
        P.wait("scalar", "QS", v_y2)
        P.wait("scalar", "ST", out_free[sl])
        P.wait("scalar", "SE", se_stats2)
        act(afv(f_sq2, 1024), abv(o_y2[sl], 1024), AF.Square,
            accum=afv(f_c2 + 1, 1))
        y2_free[sl] = act(afv(f_out[sl], 1024), abv(o_y2[sl], 1024), AF.Copy,
                          accum=afv(f_c2 + 0, 1))
        se_stats2 = y2_free[sl]
        P.wait("vector", "SE", se_stats2)
        vop(lambda e: e.tensor_scalar_mul(afv(f_c2 + 2, 1), afv(f_c2 + 0, 1),
                                          1.0 / C))
        P.wait("vector", "VE")
        vop(lambda e: e.tensor_mul(afv(f_c2 + 3, 1), afv(f_c2 + 2, 1),
                                   afv(f_c2 + 2, 1)))
        vop(lambda e: e.tensor_scalar_mul(afv(f_c2 + 4, 1), afv(f_c2 + 1, 1),
                                          1.0 / C))
        P.wait("vector", "VE")
        vop(lambda e: e.scalar_tensor_tensor(afv(f_c2 + 5, 1), afv(f_c2 + 4, 1),
                                             EPS, afv(f_c2 + 3, 1),
                                             ALU.add, ALU.subtract))
        ve_v2 = P.cnt["VE"]
        P.wait("scalar", "VE", ve_v2)
        se_sq2 = act(afv(f_c2 + 6, 1), afv(f_c2 + 5, 1), AF.Sqrt)
        P.wait("vector", "SE", se_sq2)
        vop(lambda e: e.reciprocal(afv(f_c2 + 7, 1), afv(f_c2 + 6, 1)))
        P.wait("vector", "VE")
        ve_out = vop(lambda e, s=sl: e.tensor_scalar(
            afv(f_out[s], 1024), afv(f_out[s], 1024),
            afv(f_c2 + 2, 1), afv(f_c2 + 7, 1), ALU.subtract, ALU.mult))
        P.wait("sync", "VE", ve_out)
        dma("sync", d_out[tt * 128:(tt + 1) * 128, :], afv(f_out[sl], 1024), "ST")
        out_free[sl] = P.cnt["ST"]

    P.wait("sync", "ST", P.cnt["ST"])

    with nc.Block() as block:
        for eng in ENGS:
            dec = getattr(block, eng)
            lst = P.ops[eng]

            def body(e, lst=lst):
                for fn in lst:
                    fn(e)
            dec(body)

    es.close()
    return nc


def kernel(L0_attn_x, L0_attn_Wqkv, L0_attn_Wproj, L0_attn_cascade_w,
           L0_ffn_wr, L0_ffn_br, L0_ffn_w1, L0_ffn_b1, L0_ffn_w2, L0_ffn_b2):
    x = np.ascontiguousarray(np.asarray(L0_attn_x, dtype=np.float32))
    w = _arrange_weights(np.asarray(L0_attn_Wqkv, np.float32),
                         np.asarray(L0_attn_Wproj, np.float32),
                         np.asarray(L0_attn_cascade_w, np.float32),
                         np.asarray(L0_ffn_wr, np.float32),
                         np.asarray(L0_ffn_br, np.float32),
                         np.asarray(L0_ffn_w1, np.float32),
                         np.asarray(L0_ffn_b1, np.float32),
                         np.asarray(L0_ffn_w2, np.float32),
                         np.asarray(L0_ffn_b2, np.float32))
    cst = _host_constants()
    if "nc" not in _CACHE:
        _CACHE["nc"] = build()
    nc = _CACHE["nc"]
    shared = dict(wqk=w["wqk"], wv=w["wv"], wproj=w["wproj"], casc=w["casc"],
                  wr_arr=w["wr_arr"], br_arr=w["br_arr"], w1=w["w1"],
                  b1_arr=w["b1_arr"], w2=w["w2"], b2_arr=w["b2_arr"],
                  rope_a=cst["rope_a"], rope_b=cst["rope_b"],
                  dmask=cst["dmask"], sel65=cst["sel65"], sel8=cst["sel8"],
                  iden2=cst["iden2"], idenb=cst["idenb"], idenf=cst["idenf"],
                  ones_r=cst["ones_r"])
    in_maps = []
    for b in range(8):
        m = dict(shared)
        m["x"] = np.ascontiguousarray(x[b])
        in_maps.append(m)
    res = run_bass_kernel_spmd(nc, in_maps, core_ids=list(range(8)))
    out = np.stack([np.asarray(r["out"]) for r in res.results], axis=0)
    return out.astype(np.float32)


# version 06
# speedup vs baseline: 1.1523x; 1.0892x over previous
from contextlib import ExitStack

import numpy as np
import ml_dtypes

import concourse.bass as bass
import concourse.mybir as mybir
from concourse.bass_utils import run_bass_kernel_spmd

F32 = mybir.dt.float32
BF16 = mybir.dt.bfloat16
AF = mybir.ActivationFunctionType
ALU = mybir.AluOpType
AX = mybir.AxisListType

NT = 4096
C = 1024
NH = 16
HD = 64
NPAIR = 8
NTB = 8
TB = 512
NST = 32
QW = 512
NE = 8
FF = 4096
SHIFT = -16.0
EPS = 1e-5

BF = ml_dtypes.bfloat16

_CACHE = {}


_TABLE_PROG = r"""
import numpy as np, sys
import jax.numpy as jnp
NT, NH, HD = 4096, 16, 64
inv_freq = 1.0 / (10000.0 ** (jnp.arange(0, HD, 2, dtype=jnp.float32) / HD))
t = jnp.arange(NT, dtype=jnp.float32)
cos_t, sin_t = [], []
for i in range(NH):
    freq_scale = 10000.0 ** (i / NH)
    freqs = jnp.outer(t, inv_freq * freq_scale)
    cos_t.append(np.asarray(jnp.cos(freqs)).T.astype(np.float32))
    sin_t.append(np.asarray(jnp.sin(freqs)).T.astype(np.float32))
np.stack([np.stack(cos_t), np.stack(sin_t)]).tofile(sys.argv[1])
"""


def _rope_tables_cpu():
    import os, subprocess, sys, tempfile
    fd, path = tempfile.mkstemp(suffix=".bin")
    os.close(fd)
    try:
        env = {**os.environ, "JAX_PLATFORMS": "cpu", "PYTHONPATH": ""}
        subprocess.run([sys.executable, "-c", _TABLE_PROG, path],
                       env=env, check=True, capture_output=True, timeout=600)
        z = np.fromfile(path, dtype=np.float32).reshape(2, NH, 32, NT)
        return [z[0][i] for i in range(NH)], [z[1][i] for i in range(NH)]
    finally:
        os.unlink(path)


def _host_constants():
    if "const" in _CACHE:
        return _CACHE["const"]
    cos_t, sin_t = _rope_tables_cpu()
    rope_a = np.zeros((NPAIR, 128, NT), dtype=np.float32)
    rope_b = np.zeros((NPAIR, 128, NT), dtype=np.float32)
    for p in range(NPAIR):
        h0, h1 = 2 * p, 2 * p + 1
        rope_a[p, 0:32] = cos_t[h0]
        rope_a[p, 32:64] = cos_t[h0]
        rope_a[p, 64:96] = cos_t[h1]
        rope_a[p, 96:128] = cos_t[h1]
        rope_b[p, 0:32] = -sin_t[h0]
        rope_b[p, 32:64] = sin_t[h0]
        rope_b[p, 64:96] = -sin_t[h1]
        rope_b[p, 96:128] = sin_t[h1]

    dmask = np.zeros((128, 4 * QW), dtype=np.float32)
    for mp in range(4):
        for s in range(128):
            q0 = mp * 128 + s
            if q0 < QW:
                dmask[s, mp * QW + q0:(mp + 1) * QW] = 1.0
    sel65 = np.zeros((65, 128), dtype=np.float32)
    sel65[64, :] = 1.0
    sel8 = np.zeros((8, 8 * 128), dtype=np.float32)
    for e in range(8):
        sel8[e, e * 128:(e + 1) * 128] = 1.0
    iden2 = np.zeros((128, 64), dtype=np.float32)
    iden2[0:64] = np.eye(64, dtype=np.float32)
    iden2[64:128] = np.eye(64, dtype=np.float32)
    idenb = np.eye(128, dtype=np.float32)
    c = dict(rope_a=rope_a, rope_b=rope_b, dmask=dmask,
             sel65=sel65, sel8=sel8, iden2=iden2,
             idenb=idenb, idenf=idenb.astype(np.float32),
             ones_r=np.ones((1, 128), dtype=np.float32))
    _CACHE["const"] = c
    return c


def _arrange_weights(Wqkv, Wproj, cascade_w, wr, br, w1, b1, w2, b2):
    ev = np.arange(0, HD, 2)
    od = np.arange(1, HD, 2)
    tiles = []
    for p in range(NPAIR):
        h0, h1 = 2 * p, 2 * p + 1
        q0, q1 = h0 * HD, h1 * HD
        k0, k1 = C + h0 * HD, C + h1 * HD
        kx = np.concatenate([Wqkv[:, k0 + ev], Wqkv[:, k0 + od],
                             Wqkv[:, k1 + ev], Wqkv[:, k1 + od]], axis=1)
        ky = np.concatenate([Wqkv[:, k0 + od], Wqkv[:, k0 + ev],
                             Wqkv[:, k1 + od], Wqkv[:, k1 + ev]], axis=1)
        qx = np.concatenate([Wqkv[:, q0 + ev], Wqkv[:, q0 + od],
                             Wqkv[:, q1 + ev], Wqkv[:, q1 + od]], axis=1)
        qy = np.concatenate([Wqkv[:, q0 + od], Wqkv[:, q0 + ev],
                             Wqkv[:, q1 + od], Wqkv[:, q1 + ev]], axis=1)
        tiles.append(np.concatenate([kx, ky, qx, qy], axis=1))
    return dict(
        wqk=np.ascontiguousarray(np.concatenate(tiles, axis=1)),
        wv=np.ascontiguousarray(Wqkv[:, 2 * C:3 * C]),
        wproj=np.ascontiguousarray(Wproj),
        wr_arr=np.ascontiguousarray(
            wr.reshape(8, 128, 8).transpose(1, 0, 2).reshape(128, 64)),
        br_arr=np.ascontiguousarray(br.reshape(1, 8)),
        b1_arr=np.ascontiguousarray(b1.reshape(8, 32, 128).transpose(0, 2, 1)),
        b2_arr=np.ascontiguousarray(b2.reshape(8, 8, 128).transpose(0, 2, 1)),
        casc=np.ascontiguousarray(np.broadcast_to(
            np.asarray(cascade_w).reshape(1, 15), (128, 15))),
        w1=np.ascontiguousarray(w1), w2=np.ascontiguousarray(w2))


ENGS = ("tensor", "scalar", "vector", "sync", "gpsimd")


class Plan:
    def __init__(self, sems):
        self.sems = sems
        self.cnt = {k: 0 for k in sems}
        self.ops = {e: [] for e in ENGS}
        self.last_wait = {e: {} for e in ENGS}

    def wait(self, eng, sem, val=None):
        v = self.cnt[sem] if val is None else val
        assert v <= self.cnt[sem], f"wait on future {sem}: {v} > {self.cnt[sem]}"
        if v <= 0 or self.last_wait[eng].get(sem, -1) >= v:
            return
        self.last_wait[eng][sem] = v
        h = self.sems[sem]
        self.ops[eng].append(lambda e, h=h, v=v: e.wait_ge(h, v))

    def emit(self, eng, fn, sem=None, n=1):
        if sem is None:
            self.ops[eng].append(fn)
            return None
        h = self.sems[sem]
        self.cnt[sem] += n
        self.ops[eng].append(lambda e, fn=fn, h=h, n=n: fn(e).then_inc(h, n))
        return self.cnt[sem]

    def barrier(self):
        snap = dict(self.cnt)
        for eng in ENGS:
            for s, v in snap.items():
                self.wait(eng, s, v)


class Arena:
    def __init__(self, cols):
        self.cols = cols
        self.off = 0
        self.marks = []

    def alloc(self, n):
        o = (self.off + 15) // 16 * 16
        assert o + n <= self.cols, f"arena overflow {o}+{n}>{self.cols}"
        self.off = o + n
        return o

    def push(self):
        self.marks.append(self.off)

    def pop(self):
        self.off = self.marks.pop()


def build(dbg=(), stop=None, skip=()):
    nc = bass.Bass("TRN2", target_bir_lowering=False)

    d_x = nc.dram_tensor("x", [NT, C], F32, kind="ExternalInput")
    d_wqk = nc.dram_tensor("wqk", [C, 4096], F32, kind="ExternalInput")
    d_wv = nc.dram_tensor("wv", [C, C], F32, kind="ExternalInput")
    d_wproj = nc.dram_tensor("wproj", [C, C], F32, kind="ExternalInput")
    d_casc = nc.dram_tensor("casc", [128, 15], F32, kind="ExternalInput")
    d_wr = nc.dram_tensor("wr_arr", [128, 64], F32, kind="ExternalInput")
    d_br = nc.dram_tensor("br_arr", [1, 8], F32, kind="ExternalInput")
    d_w1 = nc.dram_tensor("w1", [NE, C, FF], F32, kind="ExternalInput")
    d_b1 = nc.dram_tensor("b1_arr", [NE, 128, 32], F32, kind="ExternalInput")
    d_w2 = nc.dram_tensor("w2", [NE, FF, C], F32, kind="ExternalInput")
    d_b2 = nc.dram_tensor("b2_arr", [NE, 128, 8], F32, kind="ExternalInput")
    d_ra = nc.dram_tensor("rope_a", [NPAIR, 128, NT], F32, kind="ExternalInput")
    d_rbt = nc.dram_tensor("rope_b", [NPAIR, 128, NT], F32, kind="ExternalInput")
    d_dm = nc.dram_tensor("dmask", [128, 4 * QW], F32, kind="ExternalInput")
    d_s65 = nc.dram_tensor("sel65", [65, 128], F32, kind="ExternalInput")
    d_s8 = nc.dram_tensor("sel8", [8, 8 * 128], F32, kind="ExternalInput")
    d_i2 = nc.dram_tensor("iden2", [128, 64], F32, kind="ExternalInput")
    d_ib = nc.dram_tensor("idenb", [128, 128], F32, kind="ExternalInput")
    d_if = nc.dram_tensor("idenf", [128, 128], F32, kind="ExternalInput")
    d_1r = nc.dram_tensor("ones_r", [1, 128], F32, kind="ExternalInput")
    d_out = nc.dram_tensor("out", [NT, C], F32, kind="ExternalOutput")

    def scratch(name, shape, dtype):
        kind = "ExternalOutput" if name in dbg else "Internal"
        return nc.dram_tensor(name, shape, dtype, kind=kind)

    d_xb = scratch("xbf", [NT, C], BF16)
    d_wqkb = scratch("wqkb", [C, 4096], BF16)
    d_wvb = scratch("wvb", [C, C], BF16)
    d_wpb = scratch("wpb", [C, C], BF16)
    d_w1b = scratch("w1b", [NE, C, FF], BF16)
    d_w2b = scratch("w2b", [NE, FF, C], BF16)
    d_attnT = scratch("attnT", [C, NT], BF16)
    d_projT = scratch("projT", [C, NT], BF16)
    d_x2n = scratch("x2n", [NT, C], BF16)
    d_y2T = scratch("y2T", [C, NT], BF16)
    d_msk = scratch("msk", [8, NT], BF16)

    es = ExitStack()
    E = es.enter_context

    AB_COLS = 88576
    AF_COLS = 6400
    ab_t = E(nc.sbuf_tensor("arena_bf", [128, AB_COLS], BF16))
    af_t = E(nc.sbuf_tensor("arena_f32", [128, AF_COLS], F32))
    ab = Arena(AB_COLS)
    af = Arena(AF_COLS)

    s_dm = E(nc.sbuf_tensor("s_dm", [128, 4 * QW], BF16))
    s_s65 = E(nc.sbuf_tensor("s_s65", [65, 128], F32))
    s_s8 = E(nc.sbuf_tensor("s_s8", [8, 8 * 128], BF16))
    s_i2 = E(nc.sbuf_tensor("s_i2", [128, 64], BF16))
    s_ib = E(nc.sbuf_tensor("s_ib", [128, 128], BF16))
    s_if = E(nc.sbuf_tensor("s_if", [128, 128], F32))
    s_1r = E(nc.sbuf_tensor("s_1r", [1, 128], F32))
    s_casc = E(nc.sbuf_tensor("s_casc", [128, 15], F32))
    s_cascT = E(nc.sbuf_tensor("s_cascT", [128, 15], F32))
    s_wr = E(nc.sbuf_tensor("s_wr", [128, 64], F32))
    s_br = E(nc.sbuf_tensor("s_br", [1, 8], F32))

    ps_a = E(nc.psum_tensor("ps_a", [128, 1024], F32))
    ps_b = E(nc.psum_tensor("ps_b", [128, 1024], F32))
    ps_c = E(nc.psum_tensor("ps_c", [128, 512], F32))
    ps_t = E(nc.psum_tensor("ps_t", [128, 1024], BF16))
    ps_d = E(nc.psum_tensor("ps_d", [128, 512], F32))
    ps_e = E(nc.psum_tensor("ps_e", [128, 512], F32))

    sems = {}
    for name in ("TE", "SE", "VE", "QS", "QA", "QG", "ST", "GE"):
        sems[name] = E(nc.semaphore(name))
    P = Plan(sems)

    ps_free = {k: ("VE", 0) for k in
               ("a0", "a1", "b0", "b1", "c", "d", "e", "t")}

    def ps_wait(eng, *keys):
        for k in keys:
            s, v = ps_free[k]
            P.wait(eng, s, v)

    def abv(off, n, p0=0, p1=128):
        return ab_t[p0:p1, off:off + n]

    def afv(off, n, p0=0, p1=128):
        return af_t[p0:p1, off:off + n]

    def dma(q, out, in_, sem, **kw):
        return P.emit(q, lambda e, o=out, i=in_, kw=kw:
                      e.dma_start(out=o, in_=i, **kw), sem, 16)

    def mm(out, lhsT, rhs, start, stop, sem=None):
        return P.emit("tensor",
                      lambda e, o=out, l=lhsT, r=rhs, st=start, sp=stop:
                      e.matmul(o, l, r, start=st, stop=sp), sem)

    def act(out, in_, func, bias=0.0, scale=1.0, accum=None):
        return P.emit("scalar",
                      lambda e, o=out, i=in_, f=func, b=bias, s=scale, a=accum:
                      e.activation(o, i, f, bias=b, scale=s, accum_out=a), "SE")

    def vop(fn):
        return P.emit("vector", fn, "VE")

    flat2 = lambda d: d.rearrange("a b -> (a b)")[:]


    def _finish():
        P.barrier()
        with nc.Block() as block:
            for eng in ENGS:
                dec = getattr(block, eng)
                lst = P.ops[eng]

                def body(e, lst=lst):
                    for fn in lst:
                        fn(e)
                dec(body)
        es.close()
        return nc

    for (dt_, st_) in [(d_s65, s_s65), (d_if, s_if), (d_1r, s_1r)]:
        dma("sync", st_[:], dt_[:], "QS")
    for (dt_, st_) in [(d_casc, s_casc), (d_wr, s_wr), (d_br, s_br)]:
        dma("scalar", st_[:], dt_[:], "QA")
    for (dt_, st_) in [(d_dm, s_dm), (d_s8, s_s8), (d_i2, s_i2), (d_ib, s_ib)]:
        dma("gpsimd", st_[:], dt_[:], "QG")
    v_const_qs = P.cnt["QS"]
    v_const_qa = P.cnt["QA"]
    v_const_qg = P.cnt["QG"]

    dma("gpsimd", flat2(d_xb[0:TB, :]), flat2(d_x[0:TB, :]), "QG")
    v_xb0 = P.cnt["QG"]
    dma("gpsimd", d_wqkb[:, 0:1024], d_wqk[:, 0:1024], "QG")
    dma("gpsimd", flat2(d_wvb), flat2(d_wv), "QG")
    v_wqv0 = P.cnt["QG"]
    dma("gpsimd", flat2(d_xb[TB:NT, :]), flat2(d_x[TB:NT, :]), "QG")
    v_xb = P.cnt["QG"]
    dma("gpsimd", d_wqkb[:, 1024:4096], d_wqk[:, 1024:4096], "QG")
    v_wqv = P.cnt["QG"]
    dma("gpsimd", flat2(d_wpb), flat2(d_wproj), "QG")
    v_wpb = P.cnt["QG"]
    v_w1b, v_w2b = [], []
    for e_ in range(NE):
        dma("gpsimd", flat2(d_w1b[e_]), flat2(d_w1[e_]), "QG")
        v_w1b.append(P.cnt["QG"])
        dma("gpsimd", flat2(d_w2b[e_]), flat2(d_w2[e_]), "QG")
        v_w2b.append(P.cnt["QG"])

    P.wait("vector", "QA", v_const_qa)
    vop(lambda e: e.tensor_scalar_mul(s_cascT[:], s_casc[:], 1.0 / NT))
    if stop == "phase0":
        return _finish()

    ab.push()
    af.push()
    o_P = ab.alloc(NST * 1024)
    o_rotK = ab.alloc(NT)
    o_rotQ = ab.alloc(NT)
    o_scr = [ab.alloc(5 * TB) for _ in range(2)]
    o_tmp1 = ab.alloc(TB)
    o_tmp2 = ab.alloc(TB)
    o_V = [ab.alloc(NST * 65) for _ in range(2)]
    o_attn = [ab.alloc(NT) for _ in range(2)]
    o_xc = [ab.alloc(8 * TB) for _ in range(3)]
    o_wq = [ab.alloc(5 * 1024) for _ in range(2)]
    o_rbb = ab.alloc(QW)
    o_bps = ab.alloc(QW)
    o_tab = [ab.alloc(2 * TB) for _ in range(2)]
    f_o65 = af.alloc(2 * QW)
    f_bparts = af.alloc(256)
    f_bsums = af.alloc(32)
    f_bias = af.alloc(32)
    f_rtmp = af.alloc(QW)

    Vv = [ab_t[:, o_V[s]:o_V[s] + NST * 65].rearrange("p (m v) -> p m v", v=65)
          for s in range(2)]

    P.wait("vector", "QS", v_const_qs)
    P.wait("vector", "QG", v_const_qg)
    vop(lambda e: e.memset(afv(f_bias, 32), SHIFT))
    bias_m = [P.cnt["VE"]] * NST
    for s in range(2):
        vop(lambda e, s=s: e.memset(Vv[s][:, :, 64:65], 1.0))
    bias_ready_ve = P.cnt["VE"]

    xc_free = [0, 0, 0]
    wq_free = [0, 0]
    tab_free = [0, 0]
    scr_free_te = [0, 0]
    ve_p_m = [("VE", 0)] * NST
    attn_store = [0, 0]
    o65_te_user = 0
    pair_end_te = 0

    for pair in range(NPAIR):
        wsl = pair % 2
        P.wait("sync", "QG", v_wqv0 if pair < 2 else v_wqv)
        P.wait("sync", "TE", wq_free[wsl])
        wq_srcs = [(d_wqkb, pair * 512), (d_wqkb, pair * 512 + 128),
                   (d_wqkb, pair * 512 + 256), (d_wqkb, pair * 512 + 384),
                   (d_wvb, pair * 128)]
        for ti, (dsrc, coff) in enumerate(wq_srcs):
            src = dsrc.rearrange("(cc p) o -> p cc o", p=128)[:, :, coff:coff + 128]
            dma("sync", abv(o_wq[wsl] + ti * 1024, 1024), src, "QS")
        v_wq = P.cnt["QS"]

        for tb in range(NTB):
            sl = tb % 2
            sl3 = tb % 3
            P.wait("sync", "QG", v_xb0 if (pair == 0 and tb == 0) else v_xb)
            P.wait("sync", "TE", xc_free[sl3])
            for cc in range(8):
                dma("sync", abv(o_xc[sl3] + cc * TB, TB),
                    d_xb[tb * TB:(tb + 1) * TB, cc * 128:(cc + 1) * 128],
                    "QS", transpose=True)
            v_xc = P.cnt["QS"]
            P.wait("gpsimd", "VE", tab_free[sl])
            dma("gpsimd", abv(o_tab[sl], TB),
                d_ra[pair][:, tb * TB:(tb + 1) * TB], "QG")
            dma("gpsimd", abv(o_tab[sl] + TB, TB),
                d_rbt[pair][:, tb * TB:(tb + 1) * TB], "QG")
            v_tab = P.cnt["QG"]

            P.wait("tensor", "QS", v_wq)
            P.wait("tensor", "QS", v_xc)
            for ti in range(5) if "qkv" not in skip else []:
                half = ti % 2
                key = "a%d" % half
                pv = ps_a[:, half * 512:(half + 1) * 512]
                ps_wait("tensor", key)
                for cc in range(8):
                    mm(pv, abv(o_wq[wsl] + ti * 1024 + cc * 128, 128),
                       abv(o_xc[sl3] + cc * TB, TB),
                       start=(cc == 0), stop=(cc == 7),
                       sem="TE" if cc == 7 else None)
                te_done = P.cnt["TE"]
                P.wait("scalar", "TE", te_done)
                P.wait("scalar", "TE", scr_free_te[sl])
                se_ev = act(abv(o_scr[sl] + ti * TB, TB), pv, AF.Copy)
                ps_free[key] = ("SE", se_ev)
            xc_free[sl3] = P.cnt["TE"]
            if tb == NTB - 1:
                wq_free[wsl] = P.cnt["TE"]

            P.wait("vector", "QG", v_tab)
            P.wait("vector", "SE", P.cnt["SE"])
            P.wait("vector", "TE", pair_end_te)
            for kq in range(2) if "rot" not in skip else []:
                sx = o_scr[sl] + (2 * kq) * TB
                sy = o_scr[sl] + (2 * kq + 1) * TB
                dst = (o_rotK if kq == 0 else o_rotQ) + tb * TB
                P.wait("vector", "VE")
                vop(lambda e, a=sx, t=o_tab[sl]:
                    e.tensor_mul(abv(o_tmp1, TB), abv(a, TB), abv(t, TB)))
                vop(lambda e, a=sy, t=o_tab[sl] + TB:
                    e.tensor_mul(abv(o_tmp2, TB), abv(a, TB), abv(t, TB)))
                P.wait("vector", "VE")
                vop(lambda e, d=dst:
                    e.tensor_add(abv(d, TB), abv(o_tmp1, TB), abv(o_tmp2, TB)))
            tab_free[sl] = P.cnt["VE"]

            if "vtr" in skip:
                if stop == "tb0" and tb == 0:
                    return _finish()
                continue
            ps_wait("tensor", "t")
            P.wait("tensor", "SE", P.cnt["SE"])
            P.wait("tensor", "QS", v_const_qs)
            P.wait("tensor", "QG", v_const_qg)
            for side in range(2):
                for t4 in range(4):
                    slot = side * 4 + t4
                    P.emit("tensor",
                           lambda e, o=ps_t[:, slot * 64:slot * 64 + 64],
                           i=abv(o_scr[sl] + 4 * TB + t4 * 128, 128,
                                 side * 64, side * 64 + 64),
                           idn=s_i2[side * 64:side * 64 + 64, 0:64]:
                           e.transpose(o, i, idn), "TE")
                    P.wait("tensor", "TE")
            te_tr = P.cnt["TE"]
            scr_free_te[sl] = te_tr
            P.wait("vector", "TE", te_tr)
            P.wait("vector", "TE", pair_end_te)
            pst0 = ps_t[:, 0:256].rearrange("p (s x) -> p s x", x=64)
            pst1 = ps_t[:, 256:512].rearrange("p (s x) -> p s x", x=64)
            for side in range(2):
                src = pst0 if side == 0 else pst1
                ve_v = vop(
                    lambda e, s=src, d=Vv[side][:, tb * 4:tb * 4 + 4, 0:64]:
                    e.tensor_copy(d, s))
            ps_free["t"] = ("VE", ve_v)
            if stop == "tb0" and tb == 0:
                return _finish()

        rot_ready_ve = P.cnt["VE"]
        if stop == "pair0qkv" and pair == 0:
            return _finish()

        for side in range(2):
            h = 2 * pair + side
            b0 = 64 * side

            for qp in range(4):
                qb_e, qb_o = 2 * qp, 2 * qp + 1
                for m_ in range(8 * qp + 8):
                    both = m_ <= 8 * qp + 3
                    key = "a0" if m_ % 2 == 0 else "b0"
                    psu = ps_a if m_ % 2 == 0 else ps_b
                    P.wait("tensor", "VE", rot_ready_ve)
                    ps_wait("tensor", key)
                    if both:
                        mm(psu[:, 0:512],
                           abv(o_rotK + m_ * 128, 128, b0, b0 + 64),
                           abv(o_rotQ + qb_e * QW, QW, b0, b0 + 64),
                           start=True, stop=True)
                    mm(psu[:, 512:1024],
                       abv(o_rotK + m_ * 128, 128, b0, b0 + 64),
                       abv(o_rotQ + qb_o * QW, QW, b0, b0 + 64),
                       start=True, stop=True, sem="TE")
                    te_s = P.cnt["TE"]
                    P.wait("scalar", "TE", te_s)
                    P.wait("scalar", ve_p_m[m_][0], ve_p_m[m_][1])
                    P.wait("scalar", "VE", bias_m[m_])
                    bias_ap = afv(f_bias + m_, 1)
                    if both:
                        se_v = act(abv(o_P + m_ * 1024, 1024), psu[:],
                                   AF.Exp, bias=bias_ap, scale=0.125)
                    else:
                        se_v = act(abv(o_P + m_ * 1024 + 512, 512),
                                   psu[:, 512:1024],
                                   AF.Exp, bias=bias_ap, scale=0.125)
                    ps_free[key] = ("SE", se_v)
                    ps_free["a1" if m_ % 2 == 0 else "b1"] = ("SE", se_v)
                    if 8 * qp <= m_ <= 8 * qp + 3:
                        P.wait("vector", "SE", se_v)
                        vop(lambda e, p=abv(o_P + m_ * 1024, 512),
                            dmk=s_dm[:, (m_ - 8 * qp) * QW:(m_ - 8 * qp + 1) * QW]:
                            e.tensor_mul(p, p, dmk))
                    if m_ >= 8 * qp + 4:
                        P.wait("vector", "SE", se_v)
                        vop(lambda e, p=abv(o_P + m_ * 1024 + 512, 512),
                            dmk=s_dm[:, (m_ - 8 * qp - 4) * QW:(m_ - 8 * qp - 3) * QW]:
                            e.tensor_mul(p, p, dmk))
                se_exp_done = P.cnt["SE"]
                ve_mask_done = P.cnt["VE"]

                ve_rec = {}
                for qb in (qb_e, qb_o):
                    half = 512 * (qb % 2)
                    pso, pkey = (ps_d, "d") if qb % 2 == 0 else (ps_e, "e")
                    nmt = 4 * qb + 4
                    P.wait("tensor", "SE", se_exp_done)
                    P.wait("tensor", "VE", ve_mask_done)
                    ps_wait("tensor", pkey)
                    for m_ in range(nmt):
                        mm(pso[0:65, :], Vv[side][:, m_, :],
                           abv(o_P + m_ * 1024 + half, 512),
                           start=(m_ == 0), stop=(m_ == nmt - 1),
                           sem="TE" if m_ == nmt - 1 else None)
                    te_av = P.cnt["TE"]
                    P.wait("vector", "TE", max(te_av, o65_te_user))
                    P.wait("vector", "SE", P.cnt["SE"])
                    ve_o65 = vop(lambda e, p=pso, h2=half:
                                 e.tensor_copy(afv(f_o65 + h2, QW, 0, 65),
                                               p[0:65, :]))
                    ps_free[pkey] = ("VE", ve_o65)
                    P.wait("scalar", "VE", ve_o65)
                    act(afv(f_rtmp, QW, 0, 1), afv(f_o65 + half, QW, 64, 65), AF.Ln)
                    ve_rec[qb] = act(afv(f_o65 + half, QW, 64, 65),
                                     afv(f_rtmp, QW, 0, 1), AF.Exp, scale=-1.0)
                for qb in (qb_e, qb_o):
                    half = 512 * (qb % 2)
                    nmt = 4 * qb + 4
                    ps_wait("tensor", "c")
                    P.wait("tensor", "SE", ve_rec[qb])
                    mm(ps_c[:], s_s65[:], afv(f_o65 + half, QW, 0, 65),
                       start=True, stop=True, sem="TE")
                    o65_te_user = P.cnt["TE"]
                    P.wait("vector", "TE", o65_te_user)
                    ve_rbb = vop(lambda e: e.tensor_copy(abv(o_rbb, QW), ps_c[:]))
                    ps_free["c"] = ("VE", ve_rbb)
                    P.wait("vector", "VE")
                    P.wait("vector", "QA", attn_store[side])
                    vop(lambda e, d=abv(o_attn[side] + qb * QW, QW, 0, 64), h2=half:
                        e.tensor_mul(d, afv(f_o65 + h2, QW, 0, 64),
                                     abv(o_rbb, QW, 0, 64)))
                    last_qb = (qb == 7)
                    for m_ in range(nmt):
                        ve_p_m[m_] = ("VE", vop(
                            lambda e, p=abv(o_P + m_ * 1024 + half, 512),
                            acc=afv(f_bparts + m_ * 8 + qb, 1):
                            e.scalar_tensor_tensor(
                                abv(o_bps, QW), p, 1.0, abv(o_rbb, QW),
                                ALU.mult, ALU.mult, accum_out=acc)))
                        if last_qb and h < NH - 1:
                            qbmin = max(0, (m_ - 3 + 3) // 4)
                            P.wait("vector", "VE")
                            vop(lambda e, m=m_, q0=qbmin: e.reduce_sum(
                                afv(f_bsums + m, 1),
                                afv(f_bparts + m * 8 + q0, 8 - q0), axis=AX.X))
                            P.wait("vector", "VE")
                            bias_m[m_] = vop(lambda e, m=m_, hc=h: e.tensor_scalar(
                                afv(f_bias + m, 1), afv(f_bsums + m, 1),
                                s_cascT[:, hc:hc + 1], SHIFT, ALU.mult, ALU.add))

            P.wait("scalar", "VE", P.cnt["VE"])
            dma("scalar", d_attnT[h * 64:(h + 1) * 64, :],
                abv(o_attn[side], NT, 0, 64), "QA")
            attn_store[side] = P.cnt["QA"]

            if stop == "head0" and h == 0:
                return _finish()
        if stop == "pair0" and pair == 0:
            return _finish()
        pair_end_te = P.cnt["TE"]

    v_attn_qa = P.cnt["QA"]
    if stop == "attn":
        return _finish()
    P.barrier()
    ab.pop()
    af.pop()

    ab.push()
    o_x2T = ab.alloc(8 * NT)
    o_acc = ab.alloc(8 * NT)
    o_maskT = ab.alloc(NT)

    ab.push()
    af.push()
    o_wpj = ab.alloc(8 * 1024)
    o_atc = [ab.alloc(8 * TB) for _ in range(2)]
    o_pj = [ab.alloc(TB) for _ in range(2)]

    P.wait("sync", "QG", v_wpb)
    dma("sync", abv(o_wpj, 8 * 1024),
        d_wpb.rearrange("(cc p) o -> p cc o", p=128), "QS")
    v_wpj = P.cnt["QS"]

    atc_free = [0, 0]
    pj_free = [0, 0]
    for tb in range(NTB):
        sl = tb % 2
        P.wait("sync", "QA", v_attn_qa)
        P.wait("sync", "TE", atc_free[sl])
        for cc in range(8):
            dma("sync", abv(o_atc[sl] + cc * TB, TB),
                d_attnT[cc * 128:(cc + 1) * 128, tb * TB:(tb + 1) * TB], "QS")
        v_atc = P.cnt["QS"]
        P.wait("tensor", "QS", v_wpj)
        P.wait("tensor", "QS", v_atc)
        for oc in range(8):
            half = oc % 2
            key = "a%d" % half
            pv = ps_a[:, half * 512:(half + 1) * 512]
            ps_wait("tensor", key)
            for cc in range(8):
                mm(pv, abv(o_wpj + cc * 1024 + oc * 128, 128),
                   abv(o_atc[sl] + cc * TB, TB),
                   start=(cc == 0), stop=(cc == 7),
                   sem="TE" if cc == 7 else None)
            te_pj = P.cnt["TE"]
            psl = oc % 2
            P.wait("vector", "TE", te_pj)
            P.wait("vector", "QA", pj_free[psl])
            ve_pj = vop(lambda e, d=abv(o_pj[psl], TB), p=pv: e.tensor_copy(d, p))
            ps_free[key] = ("VE", ve_pj)
            P.wait("scalar", "VE", ve_pj)
            dma("scalar", d_projT[oc * 128:(oc + 1) * 128, tb * TB:(tb + 1) * TB],
                abv(o_pj[psl], TB), "QA")
            pj_free[psl] = P.cnt["QA"]
        atc_free[sl] = P.cnt["TE"]
    v_projT_qa = P.cnt["QA"]
    if stop == "proj":
        return _finish()
    P.barrier()
    ab.pop()
    af.pop()

    ab.push()
    af.push()
    o_pn = [ab.alloc(1024) for _ in range(2)]
    o_x2b = [ab.alloc(1024) for _ in range(2)]
    o_oh = ab.alloc(16)
    f_xr = [af.alloc(1024) for _ in range(2)]
    f_x2f = af.alloc(1024)
    f_x2Tf = af.alloc(1024)
    f_y = af.alloc(1024)
    f_sq = af.alloc(1024)
    f_cols = af.alloc(16)
    f_sc = af.alloc(16)

    pn_free = [0, 0]
    xr_free = [0, 0]
    x2b_free = [0, 0]
    se_stats = 0
    x2n_store = 0
    for tt in range(32):
        sl = tt % 2
        P.wait("sync", "QA", v_projT_qa)
        P.wait("sync", "VE", pn_free[sl])
        dma("sync", abv(o_pn[sl], 1024),
            d_projT[0:1024, tt * 128:(tt + 1) * 128], "QS", transpose=True)
        v_pn = P.cnt["QS"]
        P.wait("scalar", "VE", xr_free[sl])
        dma("scalar", afv(f_xr[sl], 1024), d_x[tt * 128:(tt + 1) * 128, :], "QA")
        v_xr = P.cnt["QA"]

        P.wait("vector", "QS", v_pn)
        P.wait("vector", "QA", v_xr)
        P.wait("vector", "SE", se_stats)
        ve_y = vop(lambda e, s=sl: e.tensor_add(afv(f_y, 1024), afv(f_xr[s], 1024),
                                                abv(o_pn[s], 1024)))
        pn_free[sl] = ve_y
        xr_free[sl] = ve_y
        P.wait("scalar", "VE", ve_y)
        act(afv(f_sq, 1024), afv(f_y, 1024), AF.Square, accum=afv(f_cols + 1, 1))
        se_stats = act(afv(f_sq, 1024), afv(f_y, 1024), AF.Copy,
                       accum=afv(f_cols + 0, 1))
        P.wait("vector", "SE", se_stats)
        vop(lambda e: e.tensor_scalar_mul(afv(f_cols + 2, 1), afv(f_cols + 0, 1),
                                          1.0 / C))
        P.wait("vector", "VE")
        vop(lambda e: e.tensor_mul(afv(f_cols + 3, 1), afv(f_cols + 2, 1),
                                   afv(f_cols + 2, 1)))
        vop(lambda e: e.tensor_scalar_mul(afv(f_cols + 4, 1), afv(f_cols + 1, 1),
                                          1.0 / C))
        P.wait("vector", "VE")
        vop(lambda e: e.scalar_tensor_tensor(afv(f_cols + 5, 1), afv(f_cols + 4, 1),
                                             EPS, afv(f_cols + 3, 1),
                                             ALU.add, ALU.subtract))
        ve_var = P.cnt["VE"]
        P.wait("scalar", "VE", ve_var)
        se_sqv = act(afv(f_cols + 6, 1), afv(f_cols + 5, 1), AF.Sqrt)
        P.wait("vector", "SE", se_sqv)
        vop(lambda e: e.reciprocal(afv(f_cols + 7, 1), afv(f_cols + 6, 1)))
        P.wait("vector", "VE")
        P.wait("vector", "QS", x2b_free[sl])
        vop(lambda e, s=sl: e.tensor_scalar(abv(o_x2b[s], 1024), afv(f_y, 1024),
                                            afv(f_cols + 2, 1), afv(f_cols + 7, 1),
                                            ALU.subtract, ALU.mult))
        vop(lambda e: e.tensor_scalar(afv(f_x2f, 1024), afv(f_y, 1024),
                                      afv(f_cols + 2, 1), afv(f_cols + 7, 1),
                                      ALU.subtract, ALU.mult))
        ve_x2 = P.cnt["VE"]
        P.wait("sync", "VE", ve_x2)
        dma("sync", d_x2n[tt * 128:(tt + 1) * 128, :], abv(o_x2b[sl], 1024), "QS")
        x2b_free[sl] = P.cnt["QS"]
        x2n_store = P.cnt["QS"]

        ps_wait("tensor", "c", "d")
        P.wait("tensor", "VE", ve_x2)
        for cc in range(8):
            pq = (ps_c if cc < 4 else ps_d)[:, (cc % 4) * 128:(cc % 4 + 1) * 128]
            P.emit("tensor", lambda e, o=pq, i=afv(f_x2f + cc * 128, 128),
                   idn=s_if[:]: e.transpose(o, i, idn), "TE")
            P.wait("tensor", "TE")
        te_tp = P.cnt["TE"]
        P.wait("vector", "TE", te_tp)
        vop(lambda e: e.tensor_copy(afv(f_x2Tf, 512), ps_c[:]))
        ve_cd = vop(lambda e: e.tensor_copy(afv(f_x2Tf + 512, 512), ps_d[:]))
        ps_free["c"] = ("VE", ve_cd)
        ps_free["d"] = ("VE", ve_cd)
        ps_wait("tensor", "e")
        P.wait("tensor", "VE", ve_cd)
        for cc in range(8):
            mm(ps_e[:, 0:8], afv(f_x2Tf + cc * 128, 128), s_wr[:, cc * 8:cc * 8 + 8],
               start=(cc == 0), stop=False)
        mm(ps_e[:, 0:8], s_1r[:], s_br[:], start=False, stop=True, sem="TE")
        te_sc = P.cnt["TE"]
        P.wait("vector", "TE", te_sc)
        ve_sc = vop(lambda e: e.tensor_copy(afv(f_sc, 8), ps_e[:, 0:8]))
        ps_free["e"] = ("VE", ve_sc)
        P.wait("vector", "VE")
        vop(lambda e: e.tensor_max(afv(f_sc + 8, 4), afv(f_sc, 4),
                                   afv(f_sc + 4, 4)))
        P.wait("vector", "VE")
        vop(lambda e: e.tensor_max(afv(f_sc + 12, 2), afv(f_sc + 8, 2),
                                   afv(f_sc + 10, 2)))
        P.wait("vector", "VE")
        vop(lambda e: e.tensor_max(afv(f_sc + 14, 1), afv(f_sc + 12, 1),
                                   afv(f_sc + 13, 1)))
        P.wait("vector", "VE")
        vop(lambda e: e.tensor_single_scalar(abv(o_oh, 8), afv(f_sc, 8),
                                             afv(f_sc + 14, 1), ALU.is_ge))
        ve_oh = P.cnt["VE"]
        ps_wait("tensor", "t")
        P.wait("tensor", "VE", ve_oh)
        P.emit("tensor", lambda e: e.transpose(ps_t[0:8, 0:128], abv(o_oh, 8),
                                               s_ib[:]), "TE")
        te_mt = P.cnt["TE"]
        P.wait("vector", "TE", te_mt)
        ve_mt = vop(lambda e, t=tt: e.tensor_copy(
            abv(o_maskT + t * 128, 128, 0, 8), ps_t[0:8, 0:128]))
        ps_free["t"] = ("VE", ve_mt)
    ve_maskT = P.cnt["VE"]

    P.wait("sync", "QS", x2n_store)
    for cc in range(8):
        dma("sync", abv(o_x2T + cc * NT, NT),
            d_x2n[0:NT, cc * 128:(cc + 1) * 128], "QS", transpose=True)
    v_x2T = P.cnt["QS"]
    if "msk" in dbg:
        P.wait("sync", "VE", ve_maskT)
        dma("sync", d_msk[:], abv(o_maskT, NT, 0, 8), "QS")

    if stop == "ln1":
        return _finish()
    P.barrier()
    ab.pop()
    af.pop()

    ab.push()
    af.push()
    o_h = ab.alloc(16 * TB)
    o_w1r = [ab.alloc(1024) for _ in range(3)]
    o_w2r = [ab.alloc(2048) for _ in range(3)]
    o_mbc = [ab.alloc(TB) for _ in range(2)]
    o_ftmp = ab.alloc(TB)
    f_b1 = af.alloc(32)
    f_b2 = af.alloc(8)

    w1r_free = [0] * 3
    w2r_free = [0] * 3
    mbc_free = [0, 0]
    h_free_te = 0
    b1_free_se = 0
    b2_free_ve = 0

    for tb in range(NTB):
        for e_ in range(NE):
            esl = (tb * NE + e_) % 2
            P.wait("scalar", "SE", b1_free_se)
            P.wait("scalar", "VE", b2_free_ve)
            dma("scalar", afv(f_b1, 32), d_b1[e_], "QA")
            dma("scalar", afv(f_b2, 8), d_b2[e_], "QA")
            v_b12 = P.cnt["QA"]
            P.wait("tensor", "VE", ve_maskT)
            ps_wait("tensor", "c")
            mm(ps_c[:], s_s8[:, e_ * 128:(e_ + 1) * 128],
               abv(o_maskT + tb * TB, TB, 0, 8), start=True, stop=True,
               sem="TE")
            te_mbc = P.cnt["TE"]
            P.wait("vector", "TE", te_mbc)
            P.wait("vector", "VE", mbc_free[esl])
            ve_mbc = vop(lambda e, d=abv(o_mbc[esl], TB): e.tensor_copy(d, ps_c[:]))
            ps_free["c"] = ("VE", ve_mbc)

            for half in range(2):
                for ft in range(16):
                    wsl = ft % 3
                    P.wait("sync", "QG", v_w1b[e_])
                    P.wait("sync", "TE", w1r_free[wsl])
                    src = d_w1b[e_].rearrange("(cc p) f -> p cc f", p=128)[
                        :, :, half * 2048 + ft * 128: half * 2048 + (ft + 1) * 128]
                    dma("sync", abv(o_w1r[wsl], 1024), src, "QS")
                    v_w1 = P.cnt["QS"]
                    ph = ft % 2
                    key = "a%d" % ph
                    pv = ps_a[:, ph * 512:(ph + 1) * 512]
                    P.wait("tensor", "QS", max(v_w1, v_x2T))
                    ps_wait("tensor", key)
                    for cc in range(8):
                        mm(pv, abv(o_w1r[wsl] + cc * 128, 128),
                           abv(o_x2T + cc * NT + tb * TB, TB),
                           start=(cc == 0), stop=(cc == 7),
                           sem="TE" if cc == 7 else None)
                    te_m1 = P.cnt["TE"]
                    w1r_free[wsl] = te_m1
                    P.wait("scalar", "QA", v_b12)
                    P.wait("scalar", "TE", max(te_m1, h_free_te))
                    se_g = act(abv(o_h + ft * TB, TB), pv, AF.Gelu,
                               bias=afv(f_b1 + half * 16 + ft, 1))
                    ps_free[key] = ("SE", se_g)
                se_h = P.cnt["SE"]
                b1_free_se = se_h

                for ht in range(8):
                    wsl = ht % 3
                    P.wait("scalar", "QG", v_w2b[e_])
                    P.wait("scalar", "TE", w2r_free[wsl])
                    src = d_w2b[e_].rearrange("(fc p) hh -> p fc hh", p=128)[
                        :, half * 16:(half + 1) * 16, ht * 128:(ht + 1) * 128]
                    dma("scalar", abv(o_w2r[wsl], 2048), src, "QA")
                    v_w2 = P.cnt["QA"]
                    ph = ht % 2
                    key = "b%d" % ph
                    pv = ps_b[:, ph * 512:(ph + 1) * 512]
                    P.wait("tensor", "QA", v_w2)
                    P.wait("tensor", "SE", se_h)
                    ps_wait("tensor", key)
                    for fc in range(16):
                        mm(pv, abv(o_w2r[wsl] + fc * 128, 128),
                           abv(o_h + fc * TB, TB),
                           start=(fc == 0), stop=(fc == 15),
                           sem="TE" if fc == 15 else None)
                    te_m2 = P.cnt["TE"]
                    w2r_free[wsl] = te_m2
                    if ht == 7:
                        h_free_te = te_m2
                    P.wait("vector", "TE", te_m2)
                    P.wait("vector", "VE", ve_mbc)
                    accs = abv(o_acc + ht * NT + tb * TB, TB)
                    if e_ == 0 and half == 0:
                        ve_m2 = vop(lambda e, d=accs, p=pv,
                                    m=abv(o_mbc[esl], TB): e.tensor_mul(d, p, m))
                    else:
                        if half == 0:
                            vop(lambda e, p=pv, m=abv(o_mbc[esl], TB):
                                e.tensor_mul(abv(o_ftmp, TB), p, m))
                        else:
                            vop(lambda e, p=pv, m=abv(o_mbc[esl], TB),
                                b2=afv(f_b2 + ht, 1):
                                e.scalar_tensor_tensor(
                                    abv(o_ftmp, TB), p, b2, m,
                                    ALU.add, ALU.mult))
                        P.wait("vector", "VE")
                        ve_m2 = vop(lambda e, d=accs:
                                    e.tensor_add(d, d, abv(o_ftmp, TB)))
                    ps_free[key] = ("VE", ve_m2)
            mbc_free[esl] = P.cnt["VE"]
            b2_free_ve = P.cnt["VE"]

    if stop == "ffn":
        return _finish()
    P.barrier()
    ab.pop()
    af.pop()

    ab.push()
    af.push()
    o_y2 = [ab.alloc(1024) for _ in range(2)]
    f_yn = af.alloc(1024)
    f_out = [af.alloc(1024) for _ in range(2)]
    f_sq2 = af.alloc(1024)
    f_c2 = af.alloc(16)

    for ch in range(8):
        P.wait("vector", "VE")
        vop(lambda e, a=abv(o_acc + ch * NT, NT), x=abv(o_x2T + ch * NT, NT):
            e.tensor_add(a, a, x))
        ve_add = P.cnt["VE"]
        P.wait("sync", "VE", ve_add)
        dma("sync", d_y2T[ch * 128:(ch + 1) * 128, :], abv(o_acc + ch * NT, NT),
            "QS")
    v_y2T = P.cnt["QS"]

    y2_free = [0, 0]
    out_free = [0, 0]
    se_stats2 = 0
    for tt in range(32):
        sl = tt % 2
        P.wait("sync", "QS", v_y2T)
        P.wait("sync", "SE", y2_free[sl])
        dma("sync", abv(o_y2[sl], 1024),
            d_y2T[0:1024, tt * 128:(tt + 1) * 128], "QS", transpose=True)
        v_y2 = P.cnt["QS"]
        P.wait("scalar", "QS", v_y2)
        P.wait("scalar", "ST", out_free[sl])
        P.wait("scalar", "SE", se_stats2)
        act(afv(f_sq2, 1024), abv(o_y2[sl], 1024), AF.Square,
            accum=afv(f_c2 + 1, 1))
        y2_free[sl] = act(afv(f_out[sl], 1024), abv(o_y2[sl], 1024), AF.Copy,
                          accum=afv(f_c2 + 0, 1))
        se_stats2 = y2_free[sl]
        P.wait("vector", "SE", se_stats2)
        vop(lambda e: e.tensor_scalar_mul(afv(f_c2 + 2, 1), afv(f_c2 + 0, 1),
                                          1.0 / C))
        P.wait("vector", "VE")
        vop(lambda e: e.tensor_mul(afv(f_c2 + 3, 1), afv(f_c2 + 2, 1),
                                   afv(f_c2 + 2, 1)))
        vop(lambda e: e.tensor_scalar_mul(afv(f_c2 + 4, 1), afv(f_c2 + 1, 1),
                                          1.0 / C))
        P.wait("vector", "VE")
        vop(lambda e: e.scalar_tensor_tensor(afv(f_c2 + 5, 1), afv(f_c2 + 4, 1),
                                             EPS, afv(f_c2 + 3, 1),
                                             ALU.add, ALU.subtract))
        ve_v2 = P.cnt["VE"]
        P.wait("scalar", "VE", ve_v2)
        se_sq2 = act(afv(f_c2 + 6, 1), afv(f_c2 + 5, 1), AF.Sqrt)
        P.wait("vector", "SE", se_sq2)
        vop(lambda e: e.reciprocal(afv(f_c2 + 7, 1), afv(f_c2 + 6, 1)))
        P.wait("vector", "VE")
        ve_out = vop(lambda e, s=sl: e.tensor_scalar(
            afv(f_out[s], 1024), afv(f_out[s], 1024),
            afv(f_c2 + 2, 1), afv(f_c2 + 7, 1), ALU.subtract, ALU.mult))
        P.wait("sync", "VE", ve_out)
        dma("sync", d_out[tt * 128:(tt + 1) * 128, :], afv(f_out[sl], 1024), "ST")
        out_free[sl] = P.cnt["ST"]

    P.wait("sync", "ST", P.cnt["ST"])

    with nc.Block() as block:
        for eng in ENGS:
            dec = getattr(block, eng)
            lst = P.ops[eng]

            def body(e, lst=lst):
                for fn in lst:
                    fn(e)
            dec(body)

    es.close()
    return nc


def kernel(L0_attn_x, L0_attn_Wqkv, L0_attn_Wproj, L0_attn_cascade_w,
           L0_ffn_wr, L0_ffn_br, L0_ffn_w1, L0_ffn_b1, L0_ffn_w2, L0_ffn_b2):
    x = np.ascontiguousarray(np.asarray(L0_attn_x, dtype=np.float32))
    w = _arrange_weights(np.asarray(L0_attn_Wqkv, np.float32),
                         np.asarray(L0_attn_Wproj, np.float32),
                         np.asarray(L0_attn_cascade_w, np.float32),
                         np.asarray(L0_ffn_wr, np.float32),
                         np.asarray(L0_ffn_br, np.float32),
                         np.asarray(L0_ffn_w1, np.float32),
                         np.asarray(L0_ffn_b1, np.float32),
                         np.asarray(L0_ffn_w2, np.float32),
                         np.asarray(L0_ffn_b2, np.float32))
    cst = _host_constants()
    if "nc" not in _CACHE:
        _CACHE["nc"] = build()
    nc = _CACHE["nc"]
    shared = dict(wqk=w["wqk"], wv=w["wv"], wproj=w["wproj"], casc=w["casc"],
                  wr_arr=w["wr_arr"], br_arr=w["br_arr"], w1=w["w1"],
                  b1_arr=w["b1_arr"], w2=w["w2"], b2_arr=w["b2_arr"],
                  rope_a=cst["rope_a"], rope_b=cst["rope_b"],
                  dmask=cst["dmask"], sel65=cst["sel65"], sel8=cst["sel8"],
                  iden2=cst["iden2"], idenb=cst["idenb"], idenf=cst["idenf"],
                  ones_r=cst["ones_r"])
    in_maps = []
    for b in range(8):
        m = dict(shared)
        m["x"] = np.ascontiguousarray(x[b])
        in_maps.append(m)
    res = run_bass_kernel_spmd(nc, in_maps, core_ids=list(range(8)))
    out = np.stack([np.asarray(r["out"]) for r in res.results], axis=0)
    return out.astype(np.float32)
